# version 17
# baseline (speedup 1.0000x reference)
import numpy as np
import ml_dtypes
from contextlib import ExitStack

import concourse.bass as bass
import concourse.mybir as mybir
from concourse.bass_utils import run_bass_kernel_spmd

F32 = mybir.dt.float32
BF16 = mybir.dt.bfloat16
AF = mybir.ActivationFunctionType
ALU = mybir.AluOpType
BF16NP = ml_dtypes.bfloat16


class CFG:
    S = 4096
    H = 2048
    INTER = 8192
    B = 4
    IB = 512
    TCK0 = 1024
    CSHIFT = 12.0
    EPS = 1e-5

    @classmethod
    def derived(cls):
        cls.SQ = cls.S // 2
        cls.HC = cls.H // 128
        cls.NH = cls.HC
        cls.KT = cls.S // 128
        cls.LT = cls.SQ // 128
        cls.QC = cls.SQ // 512
        cls.NB = cls.INTER // cls.IB
        cls.TCK = min(cls.TCK0, cls.SQ)
        cls.NCH = cls.SQ // cls.TCK


def zig(g, n_local):
    out = []
    for i in range(n_local):
        if g == 0:
            out.append(2 * i + (0 if i % 2 == 0 else 1))
        else:
            out.append(2 * i + (1 if i % 2 == 0 else 0))
    return out


def build_maskM(g):
    m = np.zeros((8, 128, 512), dtype=np.float32)
    for t in range(4):
        grel = 2 * t + ((t % 2) if g == 0 else (1 - t % 2))
        for v in range(8):
            blk = m[v][:, 128 * t:128 * (t + 1)]
            if v < grel:
                blk[:, :] = 1.0
            elif v == grel:
                kk = np.arange(128)[:, None]
                qq = np.arange(128)[None, :]
                blk[:, :] = (kk <= qq).astype(np.float32)
    return m.astype(BF16NP)



ENGINES = ("tensor", "vector", "scalar", "sync", "gpsimd")


class Sem:
    def __init__(self, h, name):
        self.h = h
        self.name = name
        self.n = 0


class Prog:
    def __init__(self, nc, stack):
        self.nc = nc
        self.stack = stack
        self.ops = {k: [] for k in ENGINES}
        self.waited = {k: {} for k in ENGINES}
        self.scope = "init"

    def sem(self, name):
        return Sem(self.stack.enter_context(self.nc.semaphore(name)), name)

    def emit(self, eng, fn):
        self.ops[eng].append((self.scope, fn))

    def wait(self, eng, sem, val):
        if sem is None or val is None or val <= 0:
            return
        if self.waited[eng].get(sem.name, 0) >= val:
            return
        self.waited[eng][sem.name] = val
        h = sem.h
        self.emit(eng, lambda e, h=h, val=val: e.wait_ge(h, val))

    def waitcur(self, eng, sem):
        self.wait(eng, sem, sem.n)

    def waits(self, eng, evts):
        for ev in evts:
            if ev is not None:
                self.wait(eng, ev[0], ev[1])

    def _run(self, eng, fn, inc, inck):
        if inc is not None:
            inc.n += inck
            self.emit(eng, lambda e, fn=fn, h=inc.h, k=inck: fn(e).then_inc(h, k))
            return inc.n
        self.emit(eng, lambda e, fn=fn: fn(e))
        return None

    def mm(self, out, lhsT, rhs, start, stop, waits=(), inc=None):
        self.waits("tensor", waits)
        return self._run(
            "tensor",
            lambda e, o=out, l=lhsT, r=rhs, st=start, sp=stop:
                e.matmul(o, l, r, start=st, stop=sp),
            inc, 1)

    def act(self, out, in_, func, bias=None, scale=None,
            accum_out=None, waits=(), inc=None):
        self.waits("scalar", waits)
        kw = {}
        if bias is not None:
            kw["bias"] = bias
        if scale is not None:
            kw["scale"] = scale
        if accum_out is not None:
            kw["accum_out"] = accum_out
        return self._run(
            "scalar",
            lambda e, o=out, i=in_, f=func, kw=kw: e.activation(o, i, f, **kw),
            inc, 1)

    def ts(self, out, in0, s1, s2, op0, op1=None, accum_out=None,
           waits=(), inc=None, eng="vector"):
        self.waits(eng, waits)
        kw = {}
        if accum_out is not None:
            kw["accum_out"] = accum_out
        if op1 is not None:
            return self._run(
                eng,
                lambda e, o=out, i=in0, a=s1, b=s2, p0=op0, p1=op1, kw=kw:
                    e.tensor_scalar(o, i, a, b, p0, p1, **kw),
                inc, 1)
        return self._run(
            eng,
            lambda e, o=out, i=in0, a=s1, p0=op0, kw=kw:
                e.tensor_scalar(o, i, a, None, p0, **kw),
            inc, 1)

    def tt(self, out, in0, in1, op, waits=(), inc=None, eng="vector"):
        self.waits(eng, waits)
        return self._run(
            eng,
            lambda e, o=out, a=in0, b=in1, p=op: e.tensor_tensor(o, a, b, p),
            inc, 1)

    def copy(self, out, in_, waits=(), inc=None, eng="vector"):
        self.waits(eng, waits)
        return self._run(
            eng, lambda e, o=out, i=in_: e.tensor_copy(o, i), inc, 1)

    def recip(self, out, in_, waits=(), inc=None):
        self.waits("vector", waits)
        return self._run(
            "vector", lambda e, o=out, i=in_: e.reciprocal(o, i), inc, 1)

    def dma(self, eng, out, in_, waits=(), inc=None, transpose=False,
            max_last=None):
        self.waits(eng, waits)
        kw = {}
        if transpose:
            kw["transpose"] = True
        if max_last is not None:
            kw["max_dma_last_dim"] = max_last
        return self._run(
            eng,
            lambda e, o=out, i=in_, kw=kw: e.dma_start(out=o, in_=i, **kw),
            inc, 16)


class Banks:

    def __init__(self, tensors):
        self.t = tensors
        self.free = [None] * len(tensors)
        self.order = list(range(len(tensors)))

    def acquire(self, p, eng="tensor"):
        b = self.order.pop(0)
        if self.free[b] is not None:
            p.wait(eng, *self.free[b])
            self.free[b] = None
        return b

    def release(self, b, evt):
        self.free[b] = evt
        self.order.append(b)



def build_program(cfg):
    cfg.derived()
    S, H, SQ, HC = cfg.S, cfg.H, cfg.SQ, cfg.HC
    INTER, IB, TCK = cfg.INTER, cfg.IB, cfg.TCK
    LT, QC, NB, NCH = cfg.LT, cfg.QC, cfg.NB, cfg.NCH
    CH = H

    nc = bass.Bass("TRN2", target_bir_lowering=False)

    xa = nc.dram_tensor("xa", [S, H], F32, kind="ExternalInput").ap()
    xo = nc.dram_tensor("xo", [SQ, H], F32, kind="ExternalInput").ap()
    wqkv = nc.dram_tensor("wqkv", [3 * CH, H], F32, kind="ExternalInput").ap()
    wo = nc.dram_tensor("wo", [H, H], F32, kind="ExternalInput").ap()
    wgu = nc.dram_tensor("wgu", [2 * INTER, H], F32, kind="ExternalInput").ap()
    wd = nc.dram_tensor("wd", [H, INTER], F32, kind="ExternalInput").ap()
    maskM_in = nc.dram_tensor("maskM", [8, 128, 512], BF16, kind="ExternalInput").ap()
    ones1_in = nc.dram_tensor("ones1", [1, 128], BF16, kind="ExternalInput").ap()
    ones128_in = nc.dram_tensor("ones128", [128, 1], BF16, kind="ExternalInput").ap()
    cvec_in = nc.dram_tensor("cvec", [128, 2], F32, kind="ExternalInput").ap()
    ident_in = nc.dram_tensor("ident", [128, 128], F32, kind="ExternalInput").ap()
    out = nc.dram_tensor("out", [SQ, H], F32, kind="ExternalOutput").ap()
    dbg_on = bool(getattr(CFG, "DEBUG_XLN", False))
    if dbg_on:
        dbg_xln = nc.dram_tensor("dbg_xln", [SQ, H], F32,
                                 kind="ExternalOutput").ap()
        dbg_xlb = nc.dram_tensor("dbg_xlb", [H // 128, SQ, 128], BF16,
                                 kind="ExternalOutput").ap()

    wqkv_bf = nc.dram_tensor("wqkv_bf", [3 * CH, H], BF16)
    wqkvT_d = nc.dram_tensor("wqkvT_d", [128, (H // 128) * 3 * CH], BF16)
    wo_bf = nc.dram_tensor("wo_bf", [H, H], BF16)
    wgu_bf = nc.dram_tensor("wgu_bf", [2 * INTER, H], BF16)
    wd_bf = nc.dram_tensor("wd_bf", [H, INTER], BF16)
    Kd = nc.dram_tensor("Kd", [CH, S], BF16)
    Vs = nc.dram_tensor("Vs", [S, CH], BF16)
    Qd = nc.dram_tensor("Qd", [CH, SQ], BF16)
    xln_bf = nc.dram_tensor("xln_bf", [H // 128, SQ, 128], BF16)
    wguT_d = nc.dram_tensor("wguT_d", [128, (H // 128) * 2 * INTER], BF16)
    wdT_d = nc.dram_tensor("wdT_d", [128, (INTER // 128) * H], BF16)
    woT_d = nc.dram_tensor("woT_d", [128, (H // 128) * H], BF16)
    xlnsp = nc.dram_tensor("xlnsp", [SQ, H], F32)

    with ExitStack() as stack:
        p = Prog(nc, stack)
        e = stack.enter_context

        t_mm = p.sem("t_mm")
        v_ev = p.sem("v_ev")
        s_ac = p.sem("s_ac")
        g_cast = p.sem("g_cast")
        c_k = [p.sem("c_k0"), p.sem("c_k1")]
        c_v = [p.sem("c_v0"), p.sem("c_v1")]
        c_x = [p.sem("c_x0"), p.sem("c_x1")]
        c_o = [p.sem("c_o0"), p.sem("c_o1")]
        y_sm = p.sem("y_sm")
        y_x = p.sem("y_x")
        y_woT = p.sem("y_woT")
        y_xw = [p.sem("y_xw0"), p.sem("y_xw1")]
        y_w = [p.sem("y_w0"), p.sem("y_w1")]
        y_h = [p.sem("y_h0"), p.sem("y_h1")]
        y_d = p.sem("y_d")
        y_ph = [p.sem("y_ph0"), p.sem("y_ph1")]
        y_xl = p.sem("y_xl")
        y_ac = p.sem("y_ac")
        c_ph = [p.sem("c_ph0"), p.sem("c_ph1")]

        pst = [e(nc.psum_tensor(f"ps{i}", [128, 512], F32)) for i in range(8)]
        banks = Banks(pst)

        ones1 = e(nc.sbuf_tensor("ones1s", [1, 128], BF16))
        ones128 = e(nc.sbuf_tensor("ones128s", [128, 1], BF16))
        idb = e(nc.sbuf_tensor("idb", [128, 128], F32))
        cvec = e(nc.sbuf_tensor("cvecs", [128, 2], F32))
        negC = cvec[:, 0:1]
        epsv = cvec[:, 1:2]
        lns = {nm: [e(nc.sbuf_tensor(f"{nm}{i}", [128, 1], F32)) for i in range(2)]
               for nm in ("s1v", "s2v", "meanv", "m2v", "varv", "stdv", "rstdv")}
        ln_vfree = [None, None]
        ln_jfree = [None, None]

        def ln_block(sp, src, jnkbuf, src_ready, out_specs):
            ws0 = list(src_ready)
            if ln_vfree[sp] is not None:
                ws0.append(ln_vfree[sp])
            if ln_jfree[sp] is not None:
                ws0.append(ln_jfree[sp])
            c1 = p.act(jnkbuf, src, AF.Copy,
                       accum_out=lns["s1v"][sp][:, :], waits=ws0, inc=s_ac)
            c2 = p.act(jnkbuf, src, AF.Square,
                       accum_out=lns["s2v"][sp][:, :], waits=[(s_ac, c1)],
                       inc=s_ac)
            ln_jfree[sp] = (s_ac, c2)
            wsv = [(s_ac, c1)]
            if ln_vfree[sp] is not None:
                wsv.append(ln_vfree[sp])
            cm = p.ts(lns["meanv"][sp][:, :], lns["s1v"][sp][:, :],
                      1.0 / H, None, ALU.mult, waits=wsv, inc=v_ev)
            cm2 = p.tt(lns["m2v"][sp][:, :], lns["meanv"][sp][:, :],
                       lns["meanv"][sp][:, :], ALU.mult,
                       waits=[(v_ev, cm)], inc=v_ev)
            cva = p.ts(lns["varv"][sp][:, :], lns["s2v"][sp][:, :],
                       1.0 / H, lns["m2v"][sp][:, :], ALU.mult, ALU.subtract,
                       waits=[(s_ac, c2), (v_ev, cm2)], inc=v_ev)
            csd = p.act(lns["stdv"][sp][:, :], lns["varv"][sp][:, :], AF.Sqrt,
                        bias=epsv, waits=[(v_ev, cva)], inc=s_ac)
            crs = p.recip(lns["rstdv"][sp][:, :], lns["stdv"][sp][:, :],
                          waits=[(s_ac, csd)], inc=v_ev)
            outs = []
            prev = crs
            for buf, extra in out_specs:
                ws = [(v_ev, prev)] + list(extra)
                co = p.ts(buf, src, lns["meanv"][sp][:, :],
                          lns["rstdv"][sp][:, :], ALU.subtract, ALU.mult,
                          waits=ws, inc=v_ev)
                outs.append(co)
                prev = co
            ln_vfree[sp] = (v_ev, outs[-1])
            return outs, (s_ac, c2)

        p.scope = "p0_cast"
        def cast_all(dst, src, rows, maxrows=2048):
            p.wait("gpsimd", g_cast, g_cast.n)
            r0 = 0
            while r0 < rows:
                r1 = min(rows, r0 + maxrows)
                p.dma("gpsimd", dst[r0:r1, :], src[r0:r1, :], inc=g_cast,
                      max_last=8192)
                r0 = r1
            return g_cast.n

        p.dma("sync", ones1[:, :], ones1_in, inc=y_sm)
        p.dma("sync", cvec[:, :], cvec_in, inc=y_sm)
        p.dma("sync", ones128[:, :], ones128_in, inc=y_sm)
        p.dma("sync", idb[:, :], ident_in, inc=y_sm)
        n_small = y_sm.n

        def cast_rows(dst, src, r0, r1, maxrows=2048):
            p.wait("gpsimd", g_cast, g_cast.n)
            rr = r0
            while rr < r1:
                r2 = min(r1, rr + maxrows)
                p.dma("gpsimd", dst[rr:r2, :], src[rr:r2, :], inc=g_cast,
                      max_last=8192)
                rr = r2
            return g_cast.n

        n_qkv_k = cast_rows(wqkv_bf.ap(), wqkv, CH, 2 * CH, maxrows=1024)

        p.scope = "p1_qkv"
        WSL = 512
        with ExitStack() as st1:
            e1 = st1.enter_context
            wsl = [e1(nc.sbuf_tensor(f"wsl{pp}", [128, HC * WSL], BF16))
                   for pp in range(2)]
            stg_q = [e1(nc.sbuf_tensor(f"stg_q{i}", [128, CH], BF16))
                     for i in range(2)]

            n_wo = cast_all(wo_bf.ap(), wo, H)
            n_gu = cast_all(wgu_bf.ap(), wgu, 2 * INTER)
            n_wd = cast_all(wd_bf.ap(), wd, H, maxrows=1024)
            stg = [e1(nc.sbuf_tensor(f"stg{pp}", [128, S // 2], BF16)) for pp in range(2)]
            stgv = [e1(nc.sbuf_tensor(f"stgv{pp}", [128, 2048], BF16)) for pp in range(2)]
            wsl_free = [None, None]
            stg_free = [None, None]
            stgv_free = [None, None]

            def load_wslab(par, row0, sec):
                ws = list(n_qT_sec[sec])
                if wsl_free[par] is not None:
                    ws.append(wsl_free[par])
                    wsl_free[par] = None
                return p.dma(
                    "sync",
                    wsl[par][:, :].rearrange("p (hc o) -> p hc o", o=WSL),
                    wqkvT_r[:, :, row0:row0 + WSL],
                    waits=ws, inc=y_w[par])

            with ExitStack() as st1a:
                e1a = st1a.enter_context
                xT = [e1a(nc.sbuf_tensor(f"xT{hc}", [128, S], BF16))
                      for hc in range(HC)]
                xan = [e1a(nc.sbuf_tensor(f"xan{i}", [128, H], F32))
                       for i in range(2)]
                g_xa = [p.sem("g_xa0"), p.sem("g_xa1")]
                xan_free = [None, None]

                def transpose_in(src, n_tiles, dst, g_sem):
                    p.wait("tensor", y_sm, n_small)
                    for st in range(n_tiles):
                        par = st % 2
                        ws = [xan_free[par]] if xan_free[par] else []
                        cgx = p.dma("sync", xan[par][:, :],
                                    src[128 * st:128 * (st + 1), :],
                                    waits=ws, inc=g_sem[par])
                        p.wait("tensor", g_sem[par], cgx)
                        last = None
                        for hc in range(HC):
                            b = banks.acquire(p)
                            pslot = pst[b][:, 0:128]
                            ctr = p._run(
                                "tensor",
                                lambda e, o=pslot,
                                i=xan[par][:, 128 * hc:128 * (hc + 1)],
                                idm=idb[:, :]: e.transpose(o, i, idm),
                                t_mm, 1)
                            dstap = dst[hc][:, 128 * st:128 * (st + 1)]
                            if hc % 2 == 0:
                                cev = p.copy(dstap, pslot,
                                             waits=[(t_mm, ctr)], inc=v_ev)
                                banks.release(b, (v_ev, cev))
                            else:
                                cev = p.act(dstap, pslot, AF.Copy,
                                            waits=[(t_mm, ctr)], inc=s_ac)
                                banks.release(b, (s_ac, cev))
                            last = ctr
                        xan_free[par] = (t_mm, last)
                    return (v_ev.n, s_ac.n)

                nxe_v, nxe_s = transpose_in(xa, S // 128, xT, g_xa)
                p.wait("tensor", v_ev, nxe_v)
                p.wait("tensor", s_ac, nxe_s)

                qT_store = [0, 0]
                kk = 0
                n_qT_sec = {}
                for sec, row0 in (("k", CH), ("v", 2 * CH), ("q", 0)):
                    if sec == "k":
                        castn = n_qkv_k
                    elif sec == "v":
                        castn = cast_rows(wqkv_bf.ap(), wqkv, 2 * CH, 3 * CH,
                                          maxrows=1024)
                    else:
                        castn = cast_rows(wqkv_bf.ap(), wqkv, 0, CH, maxrows=1024)
                    for hc in range(HC):
                        par = kk % 2
                        kk += 1
                        ws = [(g_cast, castn)]
                        if qT_store[par]:
                            ws.append((c_ph[par], qT_store[par]))
                        ct = p.dma("sync", stg_q[par][:, 0:CH],
                                   wqkv_bf.ap()[row0:row0 + CH,
                                                128 * hc:128 * (hc + 1)],
                                   transpose=True, waits=ws, inc=y_ph[par])
                        cs = p.dma("gpsimd",
                                   wqkvT_d.ap()[:, hc * 3 * CH + row0:
                                                hc * 3 * CH + row0 + CH],
                                   stg_q[par][:, 0:CH],
                                   waits=[(y_ph[par], ct)], inc=c_ph[par])
                        qT_store[par] = cs
                    n_qT_sec[sec] = [(c_ph[0], c_ph[0].n), (c_ph[1], c_ph[1].n)]
                wqkvT_r = wqkvT_d.ap().rearrange("p (hc o) -> p hc o", o=3 * CH)

                for sl in range(CH // WSL):
                    par = sl % 2
                    nw = load_wslab(par, CH + WSL * sl, 'k')
                    p.wait("tensor", y_w[par], nw)
                    last_mm = None
                    for ct in range(WSL // 128):
                        gct = sl * (WSL // 128) + ct
                        for hf in range(2):
                            sp = (2 * gct + hf) % 2
                            if stg_free[sp] is not None:
                                p.wait("vector", *stg_free[sp])
                                stg_free[sp] = None
                            last_ev = None
                            for sc in range(S // 1024):
                                b = banks.acquire(p)
                                c0 = (hf * (S // 2) + 512 * sc)
                                for hc in range(HC):
                                    last_mm = p.mm(
                                        pst[b][:, :],
                                        wsl[par][:, hc * WSL + 128 * ct:
                                                 hc * WSL + 128 * (ct + 1)],
                                        xT[hc][:, c0:c0 + 512],
                                        hc == 0, hc == HC - 1,
                                        inc=t_mm if hc == HC - 1 else None)
                                last_ev = p.ts(
                                    stg[sp][:, 512 * sc:512 * (sc + 1)],
                                    pst[b][:, :], 1.0, None, ALU.mult,
                                    waits=[(t_mm, last_mm)], inc=v_ev)
                                banks.release(b, (v_ev, last_ev))
                            cd = p.dma("scalar",
                                       Kd.ap()[128 * gct:128 * (gct + 1),
                                               hf * (S // 2):
                                               (hf + 1) * (S // 2)],
                                       stg[sp][:, :],
                                       waits=[(v_ev, last_ev)], inc=c_k[sp])
                            stg_free[sp] = (c_k[sp], cd)
                    wsl_free[par] = (t_mm, last_mm)

                for sl in range(CH // WSL):
                    par = sl % 2
                    nw = load_wslab(par, 2 * CH + WSL * sl, 'v')
                    p.wait("tensor", y_w[par], nw)
                    last_mm = None
                    for stt in range(S // 128):
                        grp = stt // 4
                        sp = grp % 2
                        sub = stt % 4
                        if sub == 0 and stgv_free[sp] is not None:
                            p.wait("vector", *stgv_free[sp])
                            stgv_free[sp] = None
                        b = banks.acquire(p)
                        for hc in range(HC):
                            last_mm = p.mm(
                                pst[b][:, :],
                                xT[hc][:, 128 * stt:128 * (stt + 1)],
                                wsl[par][:, hc * WSL:(hc + 1) * WSL],
                                hc == 0, hc == HC - 1,
                                inc=t_mm if hc == HC - 1 else None)
                        cv = p.ts(stgv[sp][:, 512 * sub:512 * (sub + 1)],
                                  pst[b][:, :], 1.0, None,
                                  ALU.mult, waits=[(t_mm, last_mm)], inc=v_ev)
                        banks.release(b, (v_ev, cv))
                        if sub == 3:
                            cd = p.dma("scalar",
                                       Vs.ap()[128 * (stt - 3):128 * (stt + 1),
                                               WSL * sl:WSL * (sl + 1)]
                                       .rearrange("(t p) v -> p t v", p=128),
                                       stgv[sp][:, :].rearrange(
                                           "p (t v) -> p t v", v=WSL),
                                       waits=[(v_ev, cv)], inc=c_v[sp])
                            stgv_free[sp] = (c_v[sp], cd)
                    wsl_free[par] = (t_mm, last_mm)
                xT_free = (t_mm, t_mm.n)

            with ExitStack() as st1b:
                e1b = st1b.enter_context
                xoT = [e1b(nc.sbuf_tensor(f"xoT{hc}", [128, SQ], BF16))
                       for hc in range(HC)]
                xan = [e1b(nc.sbuf_tensor(f"xoan{i}", [128, H], F32))
                       for i in range(2)]
                g_xo = [p.sem("g_xo0"), p.sem("g_xo1")]
                xan_free = [None, None]
                p.wait("sync", *xT_free)
                noe_v, noe_s = transpose_in(xo, SQ // 128, xoT, g_xo)
                p.wait("tensor", v_ev, noe_v)
                p.wait("tensor", s_ac, noe_s)

                qscale = 1.0 / float(np.sqrt(128.0))
                for sl in range(CH // WSL):
                    par = sl % 2
                    nw = load_wslab(par, WSL * sl, 'q')
                    p.wait("tensor", y_w[par], nw)
                    last_mm = None
                    for ct in range(WSL // 128):
                        gct = sl * (WSL // 128) + ct
                        sp = gct % 2
                        if stg_free[sp] is not None:
                            p.wait("vector", *stg_free[sp])
                            stg_free[sp] = None
                        last_ev = None
                        for sc in range(SQ // 512):
                            b = banks.acquire(p)
                            for hc in range(HC):
                                last_mm = p.mm(
                                    pst[b][:, :],
                                    wsl[par][:, hc * WSL + 128 * ct:
                                             hc * WSL + 128 * (ct + 1)],
                                    xoT[hc][:, 512 * sc:512 * (sc + 1)],
                                    hc == 0, hc == HC - 1,
                                    inc=t_mm if hc == HC - 1 else None)
                            last_ev = p.ts(stg[sp][:, 512 * sc:512 * (sc + 1)],
                                           pst[b][:, :], qscale, None, ALU.mult,
                                           waits=[(t_mm, last_mm)], inc=v_ev)
                            banks.release(b, (v_ev, last_ev))
                        cd = p.dma("scalar",
                                   Qd.ap()[128 * gct:128 * (gct + 1), :],
                                   stg[sp][:, :], waits=[(v_ev, last_ev)],
                                   inc=c_k[sp])
                        stg_free[sp] = (c_k[sp], cd)
                    wsl_free[par] = (t_mm, last_mm)
                q_done_mm = t_mm.n
            n_spill = [(c_k[0], c_k[0].n), (c_k[1], c_k[1].n),
                       (c_v[0], c_v[0].n), (c_v[1], c_v[1].n)]

        p.scope = "p2_attn"
        NH = HC
        with ExitStack() as st2a:
            e2a = st2a.enter_context
            attnT = [e2a(nc.sbuf_tensor(f"attnT{dh}", [128, SQ], BF16))
                     for dh in range(NH)]
            woTcat = e2a(nc.sbuf_tensor("woTcat", [128, HC * H], BF16))

            with ExitStack() as st2b:
                e2b = st2b.enter_context
                p1_done_pre = [(t_mm, q_done_mm), (v_ev, v_ev.n),
                               (s_ac, s_ac.n)] + n_spill
                Qh = [e2b(nc.sbuf_tensor(f"Qh{i}", [128, SQ], BF16)) for i in range(2)]
                Kh = [e2b(nc.sbuf_tensor(f"Kh{i}", [128, S], BF16)) for i in range(2)]
                Vh = [e2b(nc.sbuf_tensor(f"Vh{i}", [128, S], BF16)) for i in range(2)]
                NPRT = 4
                prT = [e2b(nc.sbuf_tensor(f"prT{i}", [128, 512], BF16))
                       for i in range(NPRT)]
                maskM = [e2b(nc.sbuf_tensor(f"maskM{v}", [128, 512], BF16))
                         for v in range(8)]
                frow_f = e2b(nc.sbuf_tensor("frow_f", [1, 512], F32))
                frow_b = e2b(nc.sbuf_tensor("frow_b", [1, 512], BF16))
                fb_sb = e2b(nc.sbuf_tensor("fb_sb", [128, 512], F32))
                p.waits("sync", p1_done_pre)
                for v in range(8):
                    p.dma("sync", maskM[v][:, :], maskM_in[v], inc=y_sm)
                n_small = y_sm.n
                stg_ph = [e2b(nc.sbuf_tensor(f"stg_ph{i}", [128, 4096], BF16))
                          for i in range(2)]
                head_free = [None, None]
                prT_free = [None] * NPRT
                frow_f_free = None
                frow_b_free = None
                fb_sb_free = None

                p1_done = [(t_mm, q_done_mm), (v_ev, v_ev.n)] + n_spill

                tspec = []
                for hc in range(HC):
                    tspec.append((wo_bf.ap()[:, 128 * hc:128 * (hc + 1)],
                                  woT_d.ap()[:, hc * H:(hc + 1) * H],
                                  H, n_wo))
                GUC = min(4096, 2 * INTER)
                for hc in range(HC):
                    for ck in range((2 * INTER) // GUC):
                        tspec.append((wgu_bf.ap()[GUC * ck:GUC * (ck + 1),
                                                  128 * hc:128 * (hc + 1)],
                                      wguT_d.ap()[:, hc * 2 * INTER + GUC * ck:
                                                  hc * 2 * INTER + GUC * (ck + 1)],
                                      GUC, n_gu))
                for itg in range(INTER // 128):
                    tspec.append((wd_bf.ap()[:, 128 * itg:128 * (itg + 1)],
                                  wdT_d.ap()[:, itg * H:(itg + 1) * H],
                                  H, n_wd))
                ph_store_cnt = [0, 0]
                ph_i = [0]
                ph_done_evt = {}

                def phaseT_step(nsteps):
                    for _ in range(nsteps):
                        k = ph_i[0]
                        if k >= len(tspec):
                            return
                        ph_i[0] = k + 1
                        src, dst, rows, castn = tspec[k]
                        par = k % 2
                        ws = [(g_cast, castn)]
                        if ph_store_cnt[par]:
                            ws.append((c_ph[par], ph_store_cnt[par]))
                        ct = p.dma("sync", stg_ph[par][:, 0:rows], src,
                                   transpose=True, waits=ws, inc=y_ph[par])
                        cs = p.dma("gpsimd", dst, stg_ph[par][:, 0:rows],
                                   waits=[(y_ph[par], ct)], inc=c_ph[par])
                        ph_store_cnt[par] = cs
                        ph_done_evt[k] = (c_ph[par], cs)

                def load_head(h):
                    ph = h % 2
                    if head_free[ph] is not None:
                        p.wait("sync", *head_free[ph])
                    p.waits("sync", p1_done)
                    p.dma("sync", Qh[ph][:, :],
                          Qd.ap()[128 * h:128 * (h + 1), :], inc=y_h[ph])
                    p.dma("sync", Kh[ph][:, :],
                          Kd.ap()[128 * h:128 * (h + 1), :], inc=y_h[ph])
                    p.dma("sync", Vh[ph][:, :].rearrange("p (j d) -> p j d", d=128),
                          Vs.ap()[:, 128 * h:128 * (h + 1)]
                          .rearrange("(j p) d -> p j d", p=128), inc=y_h[ph])
                    return y_h[ph].n

                nh_loaded = [load_head(0), load_head(1)]
                p.wait("vector", y_sm, n_small)
                p.wait("tensor", y_sm, n_small)
                pending_factor = [None]

                def flush_factor():
                    if pending_factor[0] is None:
                        return
                    nonlocal frow_f_free, frow_b_free, fb_sb_free
                    hh, qq, db, ab, last_av = pending_factor[0]
                    pending_factor[0] = None
                    wsr = [(t_mm, last_av)]
                    if frow_f_free is not None:
                        wsr.append(frow_f_free)
                    cR = p.recip(frow_f[0:1, :], pst[db][0:1, 0:512],
                                 waits=wsr, inc=v_ev)
                    banks.release(db, (v_ev, cR))
                    wsc = [(v_ev, cR)]
                    if frow_b_free is not None:
                        wsc.append(frow_b_free)
                    cC = p.copy(frow_b[0:1, :], frow_f[0:1, :],
                                waits=wsc, inc=v_ev)
                    frow_f_free = (v_ev, cC)
                    fbk = banks.acquire(p)
                    cF = p.mm(pst[fbk][:, :], ones1[0:1, :], frow_b[0:1, :],
                              True, True, waits=[(v_ev, cC)], inc=t_mm)
                    frow_b_free = (t_mm, cF)
                    wsb = [(t_mm, cF)]
                    if fb_sb_free is not None:
                        wsb.append(fb_sb_free)
                    cB = p.copy(fb_sb[:, :], pst[fbk][:, :],
                                waits=wsb, inc=v_ev)
                    banks.release(fbk, (v_ev, cB))
                    cT = p.tt(attnT[hh][:, 512 * qq:512 * (qq + 1)],
                              pst[ab][:, :], fb_sb[:, :], ALU.mult,
                              waits=[(v_ev, cB)], inc=v_ev)
                    banks.release(ab, (v_ev, cT))
                    fb_sb_free = (v_ev, cT)

                for h in range(NH):
                    ph = h % 2
                    p.wait("tensor", y_h[ph], nh_loaded[h])
                    for qc in range(QC):
                        db = banks.acquire(p)
                        ab = banks.acquire(p)
                        nj = 8 * qc + 8
                        sbanks = {}

                        def mm2(j, _db=db, _ab=ab):
                            b = banks.acquire(p)
                            cS = p.mm(pst[b][:, :],
                                      Kh[ph][:, 128 * j:128 * (j + 1)],
                                      Qh[ph][:, 512 * qc:512 * (qc + 1)],
                                      True, True, inc=t_mm)
                            sbanks[j] = (b, cS)

                        mm2(0)
                        if nj > 1:
                            mm2(1)
                        last_av = None
                        for j in range(nj):
                            b, cS = sbanks.pop(j)
                            pb = j % NPRT
                            ws = [(t_mm, cS)]
                            if prT_free[pb] is not None:
                                ws.append(prT_free[pb])
                            cE = p.act(prT[pb][:, :], pst[b][:, :],
                                       AF.Exp, bias=negC,
                                       waits=ws, inc=s_ac)
                            banks.release(b, (s_ac, cE))
                            ready = (s_ac, cE)
                            if j >= 8 * qc:
                                cMk = p.tt(prT[pb][:, :], prT[pb][:, :],
                                           maskM[j - 8 * qc][:, :], ALU.mult,
                                           waits=[ready], inc=v_ev)
                                ready = (v_ev, cMk)
                            if j + 2 < nj:
                                mm2(j + 2)
                            if j == 1:
                                flush_factor()
                            p.wait("tensor", *ready)
                            p.mm(pst[db][0:1, 0:512], ones128[:, :],
                                 prT[pb][:, :], j == 0, j == nj - 1)
                            last_av = p.mm(pst[ab][:, :],
                                           Vh[ph][:, 128 * j:128 * (j + 1)],
                                           prT[pb][:, :], j == 0, j == nj - 1,
                                           inc=t_mm)
                            prT_free[pb] = (t_mm, last_av)
                        pending_factor[0] = (h, qc, db, ab, last_av)
                    head_free[ph] = (t_mm, t_mm.n)
                    if h + 2 < NH:
                        nh_loaded.append(load_head(h + 2))
                    phaseT_step(10)
                flush_factor()
                phaseT_step(len(tspec))
                n_phT = [(c_ph[0], c_ph[0].n), (c_ph[1], c_ph[1].n)]
                attn_t = t_mm.n
                attn_v = v_ev.n
                attn_s = s_ac.n
                wo_ph = [ph_done_evt[k] for k in range(HC)]
                p.waits("sync", wo_ph)
                p.dma("sync",
                      woTcat[:, :].rearrange("p (hc o) -> p hc o", o=H),
                      woT_d.ap().rearrange("p (hc o) -> p hc o", o=H),
                      inc=y_woT)
                n_woT = y_woT.n

            p.scope = "p3_wo_ln1"
            with ExitStack() as st3:
                e3 = st3.enter_context
                xow = [e3(nc.sbuf_tensor(f"xow{i}", [128, H], F32)) for i in range(2)]
                rbuf = [e3(nc.sbuf_tensor(f"rbuf{i}", [128, H], F32)) for i in range(2)]
                jnk = [e3(nc.sbuf_tensor(f"jnk{i}", [128, H], BF16)) for i in range(2)]
                xlnb = [e3(nc.sbuf_tensor(f"xlnb{i}", [128, H], BF16)) for i in range(2)]
                xlnf = [e3(nc.sbuf_tensor(f"xlnf{i}", [128, H], F32)) for i in range(2)]
                p.wait("sync", t_mm, attn_t)
                p.wait("tensor", v_ev, attn_v)

                xow_free = [None, None]
                rbuf_free = [[], []]
                stg3b_free = [None, None]
                stg3f_free = [None, None]
                for stt in range(LT):
                    sp = stt % 2
                    if xow_free[sp] is not None:
                        p.wait("sync", *xow_free[sp])
                        xow_free[sp] = None
                    cx = p.dma("sync", xow[sp][:, :],
                               xo[128 * stt:128 * (stt + 1), :], inc=y_xw[sp])
                    last_ev = None
                    for oc in range(H // 512):
                        b = banks.acquire(p)
                        p.wait("tensor", y_woT, n_woT)
                        last_mm = None
                        for dh in range(HC):
                            last_mm = p.mm(
                                pst[b][:, :],
                                attnT[dh][:, 128 * stt:128 * (stt + 1)],
                                woTcat[:, dh * H + 512 * oc:
                                       dh * H + 512 * (oc + 1)],
                                dh == 0, dh == HC - 1,
                                inc=t_mm if dh == HC - 1 else None)
                        ws = [(t_mm, last_mm), (y_xw[sp], cx)] + rbuf_free[sp]
                        last_ev = p.tt(rbuf[sp][:, 512 * oc:512 * (oc + 1)],
                                       pst[b][:, :],
                                       xow[sp][:, 512 * oc:512 * (oc + 1)],
                                       ALU.add, waits=ws, inc=v_ev)
                        banks.release(b, (v_ev, last_ev))
                    rbuf_free[sp] = []
                    ws_b = [stg3b_free[sp]] if stg3b_free[sp] else []
                    ws_f = [stg3f_free[sp]] if stg3f_free[sp] else []
                    (cb1, cb2), jfree = ln_block(
                        sp, rbuf[sp][:, :], jnk[sp][:, :],
                        [(v_ev, last_ev)],
                        [(xlnb[sp][:, :], ws_b), (xlnf[sp][:, :], ws_f)])
                    rbuf_free[sp] = [(v_ev, cb2), jfree]
                    xow_free[sp] = (v_ev, last_ev)
                    cs1 = p.dma("scalar",
                                xln_bf.ap()[:, 128 * stt:128 * (stt + 1), :]
                                .rearrange("hc p d -> p hc d"),
                                xlnb[sp][:, :].rearrange(
                                    "p (hc d) -> p hc d", d=128),
                                waits=[(v_ev, cb1)], inc=c_x[sp])
                    cs2 = p.dma("scalar",
                                xlnsp.ap()[128 * stt:128 * (stt + 1), :],
                                xlnf[sp][:, :], waits=[(v_ev, cb2)],
                                inc=c_x[sp])
                    stg3b_free[sp] = (c_x[sp], cs2)
                    stg3f_free[sp] = (c_x[sp], cs2)
                n_xln = [(c_x[0], c_x[0].n), (c_x[1], c_x[1].n)]

        p.scope = "p4_ffn"
        p4_barrier = ([(t_mm, t_mm.n), (v_ev, v_ev.n), (s_ac, s_ac.n)]
                      + n_xln + n_phT)
        ITB = IB // 128
        with ExitStack() as st4:
            e4 = st4.enter_context
            acc = [e4(nc.sbuf_tensor(f"acc{i}", [128, H], F32))
                   for i in range(TCK // 128)]
            hsb = [[e4(nc.sbuf_tensor(f"h{pp}_{it}", [128, TCK], BF16))
                    for it in range(ITB)] for pp in range(2)]
            guT = [e4(nc.sbuf_tensor(f"guT{pp}", [128, HC * 2 * IB], BF16))
                   for pp in range(2)]
            wdT = e4(nc.sbuf_tensor("wdT", [128, ITB * H], BF16))
            gact = [e4(nc.sbuf_tensor(f"gact{i}", [128, 512], F32))
                    for i in range(2)]
            ostg = e4(nc.sbuf_tensor("ostg", [128, H], F32))
            xlnT = [e4(nc.sbuf_tensor(f"xlnT{hc}", [128, TCK], BF16))
                    for hc in range(HC)]

            guT_free = [None, None]
            wdT_free = None
            hsb_free = [None, None]
            gact_free = [None, None]
            acc_free = [None] * (TCK // 128)
            xlnT_free = None
            ostg_free = None
            gu_hready = {}

            wguT_r = wguT_d.ap().rearrange("p (hc r o) -> p hc r o",
                                           r=2, o=INTER)
            guT_r = [guT[pp].ap()[:, :].rearrange(
                "p (hc r o) -> p hc r o", r=2, o=IB) for pp in range(2)]

            def load_gu(gb):
                par = gb % 2
                ws = list(n_phT)
                if guT_free[par] is not None:
                    ws.append(guT_free[par])
                return p.dma("sync", guT_r[par],
                             wguT_r[:, :, :, IB * (gb % NB):IB * (gb % NB + 1)],
                             waits=ws, inc=y_w[par])

            def load_wd(gb, wait_evt):
                ib = gb % NB
                ws = list(n_phT)
                if wait_evt is not None:
                    ws.append(wait_evt)
                return p.dma("sync", wdT[:, :],
                             wdT_d.ap()[:, ITB * ib * H:ITB * (ib + 1) * H],
                             waits=ws, inc=y_d)

            def gate_up_block(gb, n_xlnT):
                par = gb % 2
                p.wait("tensor", y_w[par], ngu[gb])
                p.wait("tensor", y_xl, n_xlnT)
                for pr in range(ITB):
                    gb_list = []
                    ub_list = []
                    for half in range(2):
                        for sc2 in range(TCK // 512):
                            b = banks.acquire(p)
                            last_mm = None
                            for hc in range(HC):
                                col0 = hc * 2 * IB + half * IB + 128 * pr
                                last_mm = p.mm(
                                    pst[b][:, :],
                                    guT[par][:, col0:col0 + 128],
                                    xlnT[hc][:, 512 * sc2:512 * (sc2 + 1)],
                                    hc == 0, hc == HC - 1,
                                    inc=t_mm if hc == HC - 1 else None)
                            (gb_list if half == 0 else ub_list).append(
                                (b, last_mm))
                    for sc2 in range(TCK // 512):
                        gbk, cg = gb_list[sc2]
                        ubk, cu = ub_list[sc2]
                        gp = sc2 % 2
                        ws = [(t_mm, cg)]
                        if gact_free[gp] is not None:
                            ws.append(gact_free[gp])
                        ca = p.act(gact[gp][:, :], pst[gbk][:, :],
                                   AF.Silu, waits=ws, inc=s_ac)
                        banks.release(gbk, (s_ac, ca))
                        ws2 = [(s_ac, ca), (t_mm, cu)]
                        if hsb_free[par] is not None:
                            ws2.append(hsb_free[par])
                        chh = p.tt(hsb[par][pr][:, 512 * sc2:512 * (sc2 + 1)],
                                   gact[gp][:, :], pst[ubk][:, :], ALU.mult,
                                   waits=ws2, inc=v_ev)
                        banks.release(ubk, (v_ev, chh))
                        gact_free[gp] = (v_ev, chh)
                gu_hready[gb] = v_ev.n

            def down_block(gb, nwd, n_act):
                par = gb % 2
                p.wait("tensor", y_d, nwd)
                p.wait("tensor", v_ev, gu_hready[gb])
                for st2 in range(TCK // 128):
                    for oc in range(H // 512):
                        b = banks.acquire(p)
                        last_mm = None
                        for it in range(ITB):
                            last_mm = p.mm(
                                pst[b][:, :],
                                hsb[par][it][:, 128 * st2:128 * (st2 + 1)],
                                wdT[:, it * H + 512 * oc:
                                    it * H + 512 * (oc + 1)],
                                it == 0, it == ITB - 1,
                                inc=t_mm if it == ITB - 1 else None)
                        ws = [(t_mm, last_mm), (y_ac, n_act)]
                        if acc_free[st2] is not None:
                            ws.append(acc_free[st2])
                        cv = p.tt(acc[st2][:, 512 * oc:512 * (oc + 1)],
                                  pst[b][:, :],
                                  acc[st2][:, 512 * oc:512 * (oc + 1)],
                                  ALU.add, waits=ws, inc=v_ev)
                        banks.release(b, (v_ev, cv))
                        acc_free[st2] = (v_ev, cv)
                hsb_free[par] = (t_mm, t_mm.n)

            ngu = []
            for chk in range(NCH):
                r0 = chk * TCK
                gb0 = chk * NB
                if chk == 0:
                    p.waits("sync", p4_barrier)
                    p.waits("scalar", p4_barrier)
                else:
                    p.wait("sync", t_mm, xlnT_free[1])
                    p.wait("scalar", t_mm, xlnT_free[1])
                ngu.append(load_gu(gb0))
                for hc in range(HC):
                    p.dma("sync", xlnT[hc][:, :],
                          xln_bf.ap()[hc, r0:r0 + TCK, :],
                          inc=y_xl, transpose=True)
                n_xlnT = y_xl.n
                for i in range(TCK // 128):
                    ws = [acc_free[i]] if acc_free[i] else []
                    p.dma("sync", acc[i][:, :],
                          xlnsp.ap()[r0 + 128 * i:r0 + 128 * (i + 1), :],
                          waits=ws, inc=y_ac)
                    acc_free[i] = None
                n_act = y_ac.n
                nwd0 = load_wd(gb0, wdT_free)
                nwds = {gb0: nwd0}
                for ib in range(NB):
                    gb = gb0 + ib
                    if ib + 1 < NB:
                        ngu.append(load_gu(gb + 1))
                    gate_up_block(gb, n_xlnT)
                    guT_free[gb % 2] = (t_mm, t_mm.n)
                    if ib >= 1:
                        down_block(gb - 1, nwds[gb - 1], n_act)
                        nwds[gb] = load_wd(gb, (t_mm, t_mm.n))
                xlnT_free = (t_mm, t_mm.n)
                down_block(gb0 + NB - 1, nwds[gb0 + NB - 1], n_act)
                wdT_free = (t_mm, t_mm.n)

                p.scope = "p5_ln2_out"
                for st2 in range(TCK // 128):
                    sp = st2 % 2
                    ws_o = [ostg_free] if ostg_free else []
                    (co,), _jf = ln_block(
                        sp, acc[st2][:, :], ostg[:, :],
                        [acc_free[st2]] + ws_o,
                        [(ostg[:, :], [])])
                    cs = p.dma("scalar",
                               out[r0 + 128 * st2:r0 + 128 * (st2 + 1), :],
                               ostg[:, :], waits=[(v_ev, co)],
                               inc=c_o[st2 % 2])
                    ostg_free = (c_o[st2 % 2], cs)
                    acc_free[st2] = (v_ev, co)
                p.scope = "p4_ffn"

        if dbg_on:
            p.waits("gpsimd", n_xln)
            p.dma("gpsimd", dbg_xln, xlnsp.ap(), inc=g_cast)
            p.dma("gpsimd", dbg_xlb, xln_bf.ap(), inc=g_cast)
            p.wait("gpsimd", g_cast, g_cast.n)
        p.scope = "final"
        p.emit("scalar", lambda e, h=c_o[0].h, v=c_o[0].n: e.wait_ge(h, v))
        p.emit("scalar", lambda e, h=c_o[1].h, v=c_o[1].n: e.wait_ge(h, v))

        with nc.Block() as block:
            def replay(eng_ops):
                def _(eng):
                    i = 0
                    n = len(eng_ops)
                    while i < n:
                        scope = eng_ops[i][0]
                        j = i
                        with nc.named_scope(scope):
                            while j < n and eng_ops[j][0] == scope:
                                eng_ops[j][1](eng)
                                j += 1
                        i = j
                return _
            block.tensor(replay(p.ops["tensor"]))
            block.vector(replay(p.ops["vector"]))
            block.scalar(replay(p.ops["scalar"]))
            block.sync(replay(p.ops["sync"]))
            block.gpsimd(replay(p.ops["gpsimd"]))

    return nc



def make_in_maps(cfg, x, w_qkv, w_o, w_gu, w_d):
    cfg.derived()
    in_maps = []
    for c in range(8):
        b, g = c // 2, c % 2
        tiles = zig(g, cfg.LT)
        rows = np.concatenate([np.arange(128 * t, 128 * (t + 1)) for t in tiles])
        in_maps.append({
            "xa": np.ascontiguousarray(x[b]),
            "xo": np.ascontiguousarray(x[b][rows]),
            "wqkv": w_qkv,
            "wo": w_o,
            "wgu": w_gu,
            "wd": w_d,
            "maskM": build_maskM(g),
            "ones1": np.ones((1, 128), dtype=BF16NP),
            "ones128": np.ones((128, 1), dtype=BF16NP),
            "cvec": np.concatenate(
                [np.full((128, 1), -cfg.CSHIFT, np.float32),
                 np.full((128, 1), cfg.EPS, np.float32)], axis=1),
            "ident": np.eye(128, dtype=np.float32),
        })
    return in_maps


def assemble(cfg, results, dtype=np.float32):
    out = np.empty((cfg.B, cfg.S, cfg.H), dtype=dtype)
    for c in range(8):
        b, g = c // 2, c % 2
        tiles = zig(g, cfg.LT)
        res = np.asarray(results[c]["out"])
        for i, t in enumerate(tiles):
            out[b, 128 * t:128 * (t + 1), :] = res[128 * i:128 * (i + 1), :]
    return out


def kernel(L0_attn_hidden_states, L0_attn_w_qkv, L0_attn_w_o,
           L0_ffn_w_gate_up, L0_ffn_w_down):
    cfg = CFG
    cfg.derived()
    x = np.ascontiguousarray(np.asarray(L0_attn_hidden_states, dtype=np.float32))
    wqkv = np.ascontiguousarray(np.asarray(L0_attn_w_qkv, dtype=np.float32))
    wo = np.ascontiguousarray(np.asarray(L0_attn_w_o, dtype=np.float32))
    wgu = np.ascontiguousarray(np.asarray(L0_ffn_w_gate_up, dtype=np.float32))
    wd = np.ascontiguousarray(np.asarray(L0_ffn_w_down, dtype=np.float32))

    nc = build_program(cfg)
    in_maps = make_in_maps(cfg, x, wqkv, wo, wgu, wd)
    res = run_bass_kernel_spmd(nc, in_maps, core_ids=list(range(8)))
    return assemble(cfg, res.results)


# version 19
# speedup vs baseline: 1.1156x; 1.1156x over previous
import numpy as np
import ml_dtypes
from contextlib import ExitStack

import concourse.bass as bass
import concourse.mybir as mybir
from concourse.bass_utils import run_bass_kernel_spmd

F32 = mybir.dt.float32
BF16 = mybir.dt.bfloat16
AF = mybir.ActivationFunctionType
ALU = mybir.AluOpType
BF16NP = ml_dtypes.bfloat16


class CFG:
    S = 4096
    H = 2048
    INTER = 8192
    B = 4
    IB = 512
    TCK0 = 1024
    CSHIFT = 12.0
    EPS = 1e-5

    @classmethod
    def derived(cls):
        cls.SQ = cls.S // 2
        cls.HC = cls.H // 128
        cls.NH = cls.HC
        cls.KT = cls.S // 128
        cls.LT = cls.SQ // 128
        cls.QC = cls.SQ // 512
        cls.NB = cls.INTER // cls.IB
        cls.TCK = min(cls.TCK0, cls.SQ)
        cls.NCH = cls.SQ // cls.TCK


def zig(g, n_local):
    out = []
    for i in range(n_local):
        if g == 0:
            out.append(2 * i + (0 if i % 2 == 0 else 1))
        else:
            out.append(2 * i + (1 if i % 2 == 0 else 0))
    return out


def build_maskM(g):
    m = np.zeros((8, 128, 512), dtype=np.float32)
    for t in range(4):
        grel = 2 * t + ((t % 2) if g == 0 else (1 - t % 2))
        for v in range(8):
            blk = m[v][:, 128 * t:128 * (t + 1)]
            if v < grel:
                blk[:, :] = 1.0
            elif v == grel:
                kk = np.arange(128)[:, None]
                qq = np.arange(128)[None, :]
                blk[:, :] = (kk <= qq).astype(np.float32)
    return m.astype(BF16NP)



ENGINES = ("tensor", "vector", "scalar", "sync", "gpsimd")


class Sem:
    def __init__(self, h, name):
        self.h = h
        self.name = name
        self.n = 0


class Prog:
    def __init__(self, nc, stack):
        self.nc = nc
        self.stack = stack
        self.ops = {k: [] for k in ENGINES}
        self.waited = {k: {} for k in ENGINES}
        self.scope = "init"

    def sem(self, name):
        return Sem(self.stack.enter_context(self.nc.semaphore(name)), name)

    def emit(self, eng, fn):
        self.ops[eng].append((self.scope, fn))

    def wait(self, eng, sem, val):
        if sem is None or val is None or val <= 0:
            return
        if self.waited[eng].get(sem.name, 0) >= val:
            return
        self.waited[eng][sem.name] = val
        h = sem.h
        self.emit(eng, lambda e, h=h, val=val: e.wait_ge(h, val))

    def waitcur(self, eng, sem):
        self.wait(eng, sem, sem.n)

    def waits(self, eng, evts):
        for ev in evts:
            if ev is not None:
                self.wait(eng, ev[0], ev[1])

    def _run(self, eng, fn, inc, inck):
        if inc is not None:
            inc.n += inck
            self.emit(eng, lambda e, fn=fn, h=inc.h, k=inck: fn(e).then_inc(h, k))
            return inc.n
        self.emit(eng, lambda e, fn=fn: fn(e))
        return None

    def mm(self, out, lhsT, rhs, start, stop, waits=(), inc=None):
        self.waits("tensor", waits)
        return self._run(
            "tensor",
            lambda e, o=out, l=lhsT, r=rhs, st=start, sp=stop:
                e.matmul(o, l, r, start=st, stop=sp),
            inc, 1)

    def act(self, out, in_, func, bias=None, scale=None,
            accum_out=None, waits=(), inc=None):
        self.waits("scalar", waits)
        kw = {}
        if bias is not None:
            kw["bias"] = bias
        if scale is not None:
            kw["scale"] = scale
        if accum_out is not None:
            kw["accum_out"] = accum_out
        return self._run(
            "scalar",
            lambda e, o=out, i=in_, f=func, kw=kw: e.activation(o, i, f, **kw),
            inc, 1)

    def ts(self, out, in0, s1, s2, op0, op1=None, accum_out=None,
           waits=(), inc=None, eng="vector"):
        self.waits(eng, waits)
        kw = {}
        if accum_out is not None:
            kw["accum_out"] = accum_out
        if op1 is not None:
            return self._run(
                eng,
                lambda e, o=out, i=in0, a=s1, b=s2, p0=op0, p1=op1, kw=kw:
                    e.tensor_scalar(o, i, a, b, p0, p1, **kw),
                inc, 1)
        return self._run(
            eng,
            lambda e, o=out, i=in0, a=s1, p0=op0, kw=kw:
                e.tensor_scalar(o, i, a, None, p0, **kw),
            inc, 1)

    def tt(self, out, in0, in1, op, waits=(), inc=None, eng="vector"):
        self.waits(eng, waits)
        return self._run(
            eng,
            lambda e, o=out, a=in0, b=in1, p=op: e.tensor_tensor(o, a, b, p),
            inc, 1)

    def copy(self, out, in_, waits=(), inc=None, eng="vector"):
        self.waits(eng, waits)
        return self._run(
            eng, lambda e, o=out, i=in_: e.tensor_copy(o, i), inc, 1)

    def recip(self, out, in_, waits=(), inc=None):
        self.waits("vector", waits)
        return self._run(
            "vector", lambda e, o=out, i=in_: e.reciprocal(o, i), inc, 1)

    def dma(self, eng, out, in_, waits=(), inc=None, transpose=False,
            max_last=None):
        self.waits(eng, waits)
        kw = {}
        if transpose:
            kw["transpose"] = True
        if max_last is not None:
            kw["max_dma_last_dim"] = max_last
        return self._run(
            eng,
            lambda e, o=out, i=in_, kw=kw: e.dma_start(out=o, in_=i, **kw),
            inc, 16)


class Banks:

    def __init__(self, tensors):
        self.t = tensors
        self.free = [None] * len(tensors)
        self.order = list(range(len(tensors)))

    def acquire(self, p, eng="tensor"):
        b = self.order.pop(0)
        if self.free[b] is not None:
            p.wait(eng, *self.free[b])
            self.free[b] = None
        return b

    def release(self, b, evt):
        self.free[b] = evt
        self.order.append(b)



def build_program(cfg):
    cfg.derived()
    S, H, SQ, HC = cfg.S, cfg.H, cfg.SQ, cfg.HC
    INTER, IB, TCK = cfg.INTER, cfg.IB, cfg.TCK
    LT, QC, NB, NCH = cfg.LT, cfg.QC, cfg.NB, cfg.NCH
    CH = H

    nc = bass.Bass("TRN2", target_bir_lowering=False)

    xa = nc.dram_tensor("xa", [S, H], F32, kind="ExternalInput").ap()
    xo = nc.dram_tensor("xo", [SQ, H], F32, kind="ExternalInput").ap()
    wqkv = nc.dram_tensor("wqkv", [3 * CH, H], F32, kind="ExternalInput").ap()
    wo = nc.dram_tensor("wo", [H, H], F32, kind="ExternalInput").ap()
    wgu = nc.dram_tensor("wgu", [2 * INTER, H], F32, kind="ExternalInput").ap()
    wd = nc.dram_tensor("wd", [H, INTER], F32, kind="ExternalInput").ap()
    maskM_in = nc.dram_tensor("maskM", [8, 128, 512], BF16, kind="ExternalInput").ap()
    ones1_in = nc.dram_tensor("ones1", [1, 128], BF16, kind="ExternalInput").ap()
    ones128_in = nc.dram_tensor("ones128", [128, 1], BF16, kind="ExternalInput").ap()
    cvec_in = nc.dram_tensor("cvec", [128, 2], F32, kind="ExternalInput").ap()
    ident_in = nc.dram_tensor("ident", [128, 128], F32, kind="ExternalInput").ap()
    out = nc.dram_tensor("out", [SQ, H], F32, kind="ExternalOutput").ap()
    dbg_on = bool(getattr(CFG, "DEBUG_XLN", False))
    if dbg_on:
        dbg_xln = nc.dram_tensor("dbg_xln", [SQ, H], F32,
                                 kind="ExternalOutput").ap()
        dbg_xlb = nc.dram_tensor("dbg_xlb", [H // 128, SQ, 128], BF16,
                                 kind="ExternalOutput").ap()

    wqkv_cb = nc.dram_tensor("wqkv_cb", [H // 128, 3 * CH, 128], BF16)
    wo_bf = nc.dram_tensor("wo_bf", [H, H], BF16)
    wgu_bf = nc.dram_tensor("wgu_bf", [2 * INTER, H], BF16)
    wd_bf = nc.dram_tensor("wd_bf", [H, INTER], BF16)
    Kd = nc.dram_tensor("Kd", [CH, S], BF16)
    Vs = nc.dram_tensor("Vs", [S, CH], BF16)
    Qd = nc.dram_tensor("Qd", [CH, SQ], BF16)
    xln_bf = nc.dram_tensor("xln_bf", [H // 128, SQ, 128], BF16)
    wguT_d = nc.dram_tensor("wguT_d", [128, (H // 128) * 2 * INTER], BF16)
    wdT_d = nc.dram_tensor("wdT_d", [128, (INTER // 128) * H], BF16)
    woT_d = nc.dram_tensor("woT_d", [128, (H // 128) * H], BF16)
    xlnsp = nc.dram_tensor("xlnsp", [SQ, H], F32)

    with ExitStack() as stack:
        p = Prog(nc, stack)
        e = stack.enter_context

        t_mm = p.sem("t_mm")
        v_ev = p.sem("v_ev")
        s_ac = p.sem("s_ac")
        g_cast = p.sem("g_cast")
        c_k = [p.sem("c_k0"), p.sem("c_k1")]
        c_v = [p.sem("c_v0"), p.sem("c_v1")]
        c_x = [p.sem("c_x0"), p.sem("c_x1")]
        c_o = [p.sem("c_o0"), p.sem("c_o1")]
        y_sm = p.sem("y_sm")
        y_x = p.sem("y_x")
        y_woT = p.sem("y_woT")
        y_xw = [p.sem("y_xw0"), p.sem("y_xw1")]
        y_w = [p.sem("y_w0"), p.sem("y_w1"), p.sem("y_w2")]
        y_h = [p.sem("y_h0"), p.sem("y_h1")]
        y_d = p.sem("y_d")
        y_ph = [p.sem("y_ph0"), p.sem("y_ph1")]
        y_xl = p.sem("y_xl")
        y_ac = p.sem("y_ac")
        c_ph = [p.sem("c_ph0"), p.sem("c_ph1")]

        pst = [e(nc.psum_tensor(f"ps{i}", [128, 512], F32)) for i in range(8)]
        banks = Banks(pst)

        ones1 = e(nc.sbuf_tensor("ones1s", [1, 128], BF16))
        ones128 = e(nc.sbuf_tensor("ones128s", [128, 1], BF16))
        idb = e(nc.sbuf_tensor("idb", [128, 128], F32))
        cvec = e(nc.sbuf_tensor("cvecs", [128, 2], F32))
        negC = cvec[:, 0:1]
        epsv = cvec[:, 1:2]
        lns = {nm: [e(nc.sbuf_tensor(f"{nm}{i}", [128, 1], F32)) for i in range(2)]
               for nm in ("s1v", "s2v", "meanv", "m2v", "varv", "stdv", "rstdv")}
        ln_vfree = [None, None]
        ln_jfree = [None, None]

        def ln_block(sp, src, jnkbuf, src_ready, out_specs):
            ws0 = list(src_ready)
            if ln_vfree[sp] is not None:
                ws0.append(ln_vfree[sp])
            if ln_jfree[sp] is not None:
                ws0.append(ln_jfree[sp])
            c1 = p.act(jnkbuf, src, AF.Copy,
                       accum_out=lns["s1v"][sp][:, :], waits=ws0, inc=s_ac)
            c2 = p.act(jnkbuf, src, AF.Square,
                       accum_out=lns["s2v"][sp][:, :], waits=[(s_ac, c1)],
                       inc=s_ac)
            ln_jfree[sp] = (s_ac, c2)
            wsv = [(s_ac, c1)]
            if ln_vfree[sp] is not None:
                wsv.append(ln_vfree[sp])
            cm = p.ts(lns["meanv"][sp][:, :], lns["s1v"][sp][:, :],
                      1.0 / H, None, ALU.mult, waits=wsv, inc=v_ev)
            cm2 = p.tt(lns["m2v"][sp][:, :], lns["meanv"][sp][:, :],
                       lns["meanv"][sp][:, :], ALU.mult,
                       waits=[(v_ev, cm)], inc=v_ev)
            cva = p.ts(lns["varv"][sp][:, :], lns["s2v"][sp][:, :],
                       1.0 / H, lns["m2v"][sp][:, :], ALU.mult, ALU.subtract,
                       waits=[(s_ac, c2), (v_ev, cm2)], inc=v_ev)
            csd = p.act(lns["stdv"][sp][:, :], lns["varv"][sp][:, :], AF.Sqrt,
                        bias=epsv, waits=[(v_ev, cva)], inc=s_ac)
            crs = p.recip(lns["rstdv"][sp][:, :], lns["stdv"][sp][:, :],
                          waits=[(s_ac, csd)], inc=v_ev)
            outs = []
            prev = crs
            for buf, extra in out_specs:
                ws = [(v_ev, prev)] + list(extra)
                co = p.ts(buf, src, lns["meanv"][sp][:, :],
                          lns["rstdv"][sp][:, :], ALU.subtract, ALU.mult,
                          waits=ws, inc=v_ev)
                outs.append(co)
                prev = co
            ln_vfree[sp] = (v_ev, outs[-1])
            return outs, (s_ac, c2)

        p.scope = "p0_cast"
        def cast_all(dst, src, rows, maxrows=2048):
            p.wait("gpsimd", g_cast, g_cast.n)
            r0 = 0
            while r0 < rows:
                r1 = min(rows, r0 + maxrows)
                p.dma("gpsimd", dst[r0:r1, :], src[r0:r1, :], inc=g_cast,
                      max_last=8192)
                r0 = r1
            return g_cast.n

        p.dma("sync", ones1[:, :], ones1_in, inc=y_sm)
        p.dma("sync", cvec[:, :], cvec_in, inc=y_sm)
        p.dma("sync", ones128[:, :], ones128_in, inc=y_sm)
        p.dma("sync", idb[:, :], ident_in, inc=y_sm)
        n_small = y_sm.n

        def cast_rows(dst, src, r0, r1, maxrows=2048):
            p.wait("gpsimd", g_cast, g_cast.n)
            rr = r0
            while rr < r1:
                r2 = min(r1, rr + maxrows)
                p.dma("gpsimd", dst[rr:r2, :], src[rr:r2, :], inc=g_cast,
                      max_last=8192)
                rr = r2
            return g_cast.n

        def cast_qkv_cb(r0, r1):
            p.wait("gpsimd", g_cast, g_cast.n)
            p.dma("gpsimd",
                  wqkv_cb.ap()[:, r0:r1, :].rearrange("hc r d -> r hc d"),
                  wqkv[r0:r1, :].rearrange("r (hc d) -> r hc d", d=128),
                  inc=g_cast)
            return g_cast.n

        n_qkv_k = [cast_qkv_cb(CH + 512 * i, CH + 512 * (i + 1))
                   for i in range(CH // 512)]
        n_qkv_v = [cast_qkv_cb(2 * CH + 512 * i, 2 * CH + 512 * (i + 1))
                   for i in range(CH // 512)]
        n_qkv_q = [cast_qkv_cb(512 * i, 512 * (i + 1))
                   for i in range(CH // 512)]
        n_wo = cast_all(wo_bf.ap(), wo, H)
        n_gu = cast_all(wgu_bf.ap(), wgu, 2 * INTER)
        n_wd = cast_all(wd_bf.ap(), wd, H, maxrows=1024)

        p.scope = "p1_qkv"
        WSL = 512
        with ExitStack() as st1:
            e1 = st1.enter_context
            NWSL = 3
            wsl = [[e1(nc.sbuf_tensor(f"wsl{pp}_{hc}", [128, WSL], BF16))
                    for hc in range(HC)] for pp in range(NWSL)]
            stg = [e1(nc.sbuf_tensor(f"stg{pp}", [128, S // 2], BF16)) for pp in range(2)]
            stgv = [e1(nc.sbuf_tensor(f"stgv{pp}", [128, 1024], BF16)) for pp in range(2)]
            wsl_free = [None] * NWSL
            stg_free = [None, None]
            stgv_free = [None, None]

            def load_wslab(par, row0, castn):
                if wsl_free[par] is not None:
                    p.wait("sync", *wsl_free[par])
                    wsl_free[par] = None
                p.wait("sync", g_cast, castn)
                for hc in range(HC):
                    p.dma("sync", wsl[par][hc][:, :],
                          wqkv_cb.ap()[hc, row0:row0 + WSL, :],
                          inc=y_w[par], transpose=True)
                return y_w[par].n

            with ExitStack() as st1a:
                e1a = st1a.enter_context
                xT = [e1a(nc.sbuf_tensor(f"xT{hc}", [128, S], BF16))
                      for hc in range(HC)]
                xan = [e1a(nc.sbuf_tensor(f"xan{i}", [128, H], F32))
                       for i in range(2)]
                g_xa = [p.sem("g_xa0"), p.sem("g_xa1")]
                xan_free = [None, None]

                def transpose_in(src, n_tiles, dst, g_sem):
                    p.wait("tensor", y_sm, n_small)
                    for st in range(n_tiles):
                        par = st % 2
                        ws = [xan_free[par]] if xan_free[par] else []
                        cgx = p.dma("sync", xan[par][:, :],
                                    src[128 * st:128 * (st + 1), :],
                                    waits=ws, inc=g_sem[par])
                        p.wait("tensor", g_sem[par], cgx)
                        last = None
                        for hc in range(HC):
                            b = banks.acquire(p)
                            pslot = pst[b][:, 0:128]
                            ctr = p._run(
                                "tensor",
                                lambda e, o=pslot,
                                i=xan[par][:, 128 * hc:128 * (hc + 1)],
                                idm=idb[:, :]: e.transpose(o, i, idm),
                                t_mm, 1)
                            dstap = dst[hc][:, 128 * st:128 * (st + 1)]
                            if hc % 2 == 0:
                                cev = p.copy(dstap, pslot,
                                             waits=[(t_mm, ctr)], inc=v_ev)
                                banks.release(b, (v_ev, cev))
                            else:
                                cev = p.act(dstap, pslot, AF.Copy,
                                            waits=[(t_mm, ctr)], inc=s_ac)
                                banks.release(b, (s_ac, cev))
                            last = ctr
                        xan_free[par] = (t_mm, last)
                    return (v_ev.n, s_ac.n)

                nxe_v, nxe_s = transpose_in(xa, S // 128, xT, g_xa)
                p.wait("tensor", v_ev, nxe_v)
                p.wait("tensor", s_ac, nxe_s)

                for sl in range(CH // WSL):
                    par = sl % NWSL
                    nw = load_wslab(par, CH + WSL * sl, n_qkv_k[sl])
                    p.wait("tensor", y_w[par], nw)
                    last_mm = None
                    for ct in range(WSL // 128):
                        gct = sl * (WSL // 128) + ct
                        for hf in range(2):
                            sp = hf
                            if stg_free[sp] is not None:
                                p.wait("vector", *stg_free[sp])
                                stg_free[sp] = None
                            last_ev = None
                            for sc in range(S // 1024):
                                b = banks.acquire(p)
                                c0 = hf * (S // 2) + 512 * sc
                                for hc in range(HC):
                                    last_mm = p.mm(
                                        pst[b][:, :],
                                        wsl[par][hc][:, 128 * ct:128 * (ct + 1)],
                                        xT[hc][:, c0:c0 + 512],
                                        hc == 0, hc == HC - 1,
                                        inc=t_mm if hc == HC - 1 else None)
                                last_ev = p.ts(
                                    stg[sp][:, 512 * sc:512 * (sc + 1)],
                                    pst[b][:, :], 1.0, None, ALU.mult,
                                    waits=[(t_mm, last_mm)], inc=v_ev)
                                banks.release(b, (v_ev, last_ev))
                            cd = p.dma("scalar",
                                       Kd.ap()[128 * gct:128 * (gct + 1),
                                               hf * (S // 2):(hf + 1) * (S // 2)],
                                       stg[sp][:, :],
                                       waits=[(v_ev, last_ev)], inc=c_k[sp])
                            stg_free[sp] = (c_k[sp], cd)
                    wsl_free[par] = (t_mm, last_mm)

                for sl in range(CH // WSL):
                    par = (CH // WSL + sl) % NWSL
                    nw = load_wslab(par, 2 * CH + WSL * sl, n_qkv_v[sl])
                    p.wait("tensor", y_w[par], nw)
                    last_mm = None
                    for stt in range(S // 128):
                        grp = stt // 2
                        sp = grp % 2
                        sub = stt % 2
                        if sub == 0 and stgv_free[sp] is not None:
                            p.wait("vector", *stgv_free[sp])
                            stgv_free[sp] = None
                        b = banks.acquire(p)
                        for hc in range(HC):
                            last_mm = p.mm(
                                pst[b][:, :],
                                xT[hc][:, 128 * stt:128 * (stt + 1)],
                                wsl[par][hc][:, :],
                                hc == 0, hc == HC - 1,
                                inc=t_mm if hc == HC - 1 else None)
                        cv = p.ts(stgv[sp][:, 512 * sub:512 * (sub + 1)],
                                  pst[b][:, :], 1.0, None,
                                  ALU.mult, waits=[(t_mm, last_mm)], inc=v_ev)
                        banks.release(b, (v_ev, cv))
                        if sub == 1:
                            cd = p.dma("scalar",
                                       Vs.ap()[128 * (stt - 1):128 * (stt + 1),
                                               WSL * sl:WSL * (sl + 1)]
                                       .rearrange("(t p) v -> p t v", p=128),
                                       stgv[sp][:, :].rearrange(
                                           "p (t v) -> p t v", v=WSL),
                                       waits=[(v_ev, cv)], inc=c_v[sp])
                            stgv_free[sp] = (c_v[sp], cd)
                    wsl_free[par] = (t_mm, last_mm)
                xT_free = (t_mm, t_mm.n)

            with ExitStack() as st1b:
                e1b = st1b.enter_context
                xoT = [e1b(nc.sbuf_tensor(f"xoT{hc}", [128, SQ], BF16))
                       for hc in range(HC)]
                xan = [e1b(nc.sbuf_tensor(f"xoan{i}", [128, H], F32))
                       for i in range(2)]
                g_xo = [p.sem("g_xo0"), p.sem("g_xo1")]
                xan_free = [None, None]
                p.wait("sync", *xT_free)
                noe_v, noe_s = transpose_in(xo, SQ // 128, xoT, g_xo)
                p.wait("tensor", v_ev, noe_v)
                p.wait("tensor", s_ac, noe_s)

                qscale = 1.0 / float(np.sqrt(128.0))
                for sl in range(CH // WSL):
                    par = (2 * (CH // WSL) + sl) % NWSL
                    nw = load_wslab(par, WSL * sl, n_qkv_q[sl])
                    p.wait("tensor", y_w[par], nw)
                    last_mm = None
                    for ct in range(WSL // 128):
                        gct = sl * (WSL // 128) + ct
                        sp = gct % 2
                        if stg_free[sp] is not None:
                            p.wait("vector", *stg_free[sp])
                            stg_free[sp] = None
                        last_ev = None
                        for sc in range(SQ // 512):
                            b = banks.acquire(p)
                            for hc in range(HC):
                                last_mm = p.mm(
                                    pst[b][:, :],
                                    wsl[par][hc][:, 128 * ct:128 * (ct + 1)],
                                    xoT[hc][:, 512 * sc:512 * (sc + 1)],
                                    hc == 0, hc == HC - 1,
                                    inc=t_mm if hc == HC - 1 else None)
                            last_ev = p.ts(stg[sp][:, 512 * sc:512 * (sc + 1)],
                                           pst[b][:, :], qscale, None, ALU.mult,
                                           waits=[(t_mm, last_mm)], inc=v_ev)
                            banks.release(b, (v_ev, last_ev))
                        cd = p.dma("scalar",
                                   Qd.ap()[128 * gct:128 * (gct + 1), :],
                                   stg[sp][:, :], waits=[(v_ev, last_ev)],
                                   inc=c_k[sp])
                        stg_free[sp] = (c_k[sp], cd)
                    wsl_free[par] = (t_mm, last_mm)
                q_done_mm = t_mm.n
            n_spill = [(c_k[0], c_k[0].n), (c_k[1], c_k[1].n),
                       (c_v[0], c_v[0].n), (c_v[1], c_v[1].n)]

        p.scope = "p2_attn"
        NH = HC
        with ExitStack() as st2a:
            e2a = st2a.enter_context
            attnT = [e2a(nc.sbuf_tensor(f"attnT{dh}", [128, SQ], BF16))
                     for dh in range(NH)]
            woTcat = e2a(nc.sbuf_tensor("woTcat", [128, HC * H], BF16))

            with ExitStack() as st2b:
                e2b = st2b.enter_context
                p1_done_pre = [(t_mm, q_done_mm), (v_ev, v_ev.n),
                               (s_ac, s_ac.n)] + n_spill
                Qh = [e2b(nc.sbuf_tensor(f"Qh{i}", [128, SQ], BF16)) for i in range(2)]
                Kh = [e2b(nc.sbuf_tensor(f"Kh{i}", [128, S], BF16)) for i in range(2)]
                Vh = [e2b(nc.sbuf_tensor(f"Vh{i}", [128, S], BF16)) for i in range(2)]
                NPRT = 4
                prT = [e2b(nc.sbuf_tensor(f"prT{i}", [128, 512], BF16))
                       for i in range(NPRT)]
                maskM = [e2b(nc.sbuf_tensor(f"maskM{v}", [128, 512], BF16))
                         for v in range(8)]
                frow_f = e2b(nc.sbuf_tensor("frow_f", [1, 512], F32))
                frow_b = e2b(nc.sbuf_tensor("frow_b", [1, 512], BF16))
                fb_sb = e2b(nc.sbuf_tensor("fb_sb", [128, 512], F32))
                p.waits("sync", p1_done_pre)
                for v in range(8):
                    p.dma("sync", maskM[v][:, :], maskM_in[v], inc=y_sm)
                n_small = y_sm.n
                stg_ph = [e2b(nc.sbuf_tensor(f"stg_ph{i}", [128, 4096], BF16))
                          for i in range(2)]
                head_free = [None, None]
                prT_free = [None] * NPRT
                frow_f_free = None
                frow_b_free = None
                fb_sb_free = None

                p1_done = [(t_mm, q_done_mm), (v_ev, v_ev.n)] + n_spill

                tspec = []
                for hc in range(HC):
                    tspec.append((wo_bf.ap()[:, 128 * hc:128 * (hc + 1)],
                                  woT_d.ap()[:, hc * H:(hc + 1) * H],
                                  H, n_wo))
                GUC = min(4096, 2 * INTER)
                for hc in range(HC):
                    for ck in range((2 * INTER) // GUC):
                        tspec.append((wgu_bf.ap()[GUC * ck:GUC * (ck + 1),
                                                  128 * hc:128 * (hc + 1)],
                                      wguT_d.ap()[:, hc * 2 * INTER + GUC * ck:
                                                  hc * 2 * INTER + GUC * (ck + 1)],
                                      GUC, n_gu))
                for itg in range(INTER // 128):
                    tspec.append((wd_bf.ap()[:, 128 * itg:128 * (itg + 1)],
                                  wdT_d.ap()[:, itg * H:(itg + 1) * H],
                                  H, n_wd))
                ph_store_cnt = [0, 0]
                ph_i = [0]
                ph_done_evt = {}

                def phaseT_step(nsteps):
                    for _ in range(nsteps):
                        k = ph_i[0]
                        if k >= len(tspec):
                            return
                        ph_i[0] = k + 1
                        src, dst, rows, castn = tspec[k]
                        par = k % 2
                        ws = [(g_cast, castn)]
                        if ph_store_cnt[par]:
                            ws.append((c_ph[par], ph_store_cnt[par]))
                        ct = p.dma("sync", stg_ph[par][:, 0:rows], src,
                                   transpose=True, waits=ws, inc=y_ph[par])
                        cs = p.dma("gpsimd", dst, stg_ph[par][:, 0:rows],
                                   waits=[(y_ph[par], ct)], inc=c_ph[par])
                        ph_store_cnt[par] = cs
                        ph_done_evt[k] = (c_ph[par], cs)

                def load_head(h):
                    ph = h % 2
                    if head_free[ph] is not None:
                        p.wait("sync", *head_free[ph])
                    p.waits("sync", p1_done)
                    p.dma("sync", Qh[ph][:, :],
                          Qd.ap()[128 * h:128 * (h + 1), :], inc=y_h[ph])
                    p.dma("sync", Kh[ph][:, :],
                          Kd.ap()[128 * h:128 * (h + 1), :], inc=y_h[ph])
                    p.dma("sync", Vh[ph][:, :].rearrange("p (j d) -> p j d", d=128),
                          Vs.ap()[:, 128 * h:128 * (h + 1)]
                          .rearrange("(j p) d -> p j d", p=128), inc=y_h[ph])
                    return y_h[ph].n

                nh_loaded = [load_head(0), load_head(1)]
                p.wait("vector", y_sm, n_small)
                p.wait("tensor", y_sm, n_small)
                pending_factor = [None]

                def flush_factor():
                    if pending_factor[0] is None:
                        return
                    nonlocal frow_f_free, frow_b_free, fb_sb_free
                    hh, qq, db, ab, last_av = pending_factor[0]
                    pending_factor[0] = None
                    wsr = [(t_mm, last_av)]
                    if frow_f_free is not None:
                        wsr.append(frow_f_free)
                    cR = p.recip(frow_f[0:1, :], pst[db][0:1, 0:512],
                                 waits=wsr, inc=v_ev)
                    banks.release(db, (v_ev, cR))
                    wsc = [(v_ev, cR)]
                    if frow_b_free is not None:
                        wsc.append(frow_b_free)
                    cC = p.copy(frow_b[0:1, :], frow_f[0:1, :],
                                waits=wsc, inc=v_ev)
                    frow_f_free = (v_ev, cC)
                    fbk = banks.acquire(p)
                    cF = p.mm(pst[fbk][:, :], ones1[0:1, :], frow_b[0:1, :],
                              True, True, waits=[(v_ev, cC)], inc=t_mm)
                    frow_b_free = (t_mm, cF)
                    wsb = [(t_mm, cF)]
                    if fb_sb_free is not None:
                        wsb.append(fb_sb_free)
                    cB = p.copy(fb_sb[:, :], pst[fbk][:, :],
                                waits=wsb, inc=v_ev)
                    banks.release(fbk, (v_ev, cB))
                    cT = p.tt(attnT[hh][:, 512 * qq:512 * (qq + 1)],
                              pst[ab][:, :], fb_sb[:, :], ALU.mult,
                              waits=[(v_ev, cB)], inc=v_ev)
                    banks.release(ab, (v_ev, cT))
                    fb_sb_free = (v_ev, cT)

                for h in range(NH):
                    ph = h % 2
                    p.wait("tensor", y_h[ph], nh_loaded[h])
                    for qc in range(QC):
                        db = banks.acquire(p)
                        ab = banks.acquire(p)
                        nj = 8 * qc + 8
                        sbanks = {}

                        def mm2(j, _db=db, _ab=ab):
                            b = banks.acquire(p)
                            cS = p.mm(pst[b][:, :],
                                      Kh[ph][:, 128 * j:128 * (j + 1)],
                                      Qh[ph][:, 512 * qc:512 * (qc + 1)],
                                      True, True, inc=t_mm)
                            sbanks[j] = (b, cS)

                        mm2(0)
                        if nj > 1:
                            mm2(1)
                        last_av = None
                        for j in range(nj):
                            b, cS = sbanks.pop(j)
                            pb = j % NPRT
                            ws = [(t_mm, cS)]
                            if prT_free[pb] is not None:
                                ws.append(prT_free[pb])
                            cE = p.act(prT[pb][:, :], pst[b][:, :],
                                       AF.Exp, bias=negC,
                                       waits=ws, inc=s_ac)
                            banks.release(b, (s_ac, cE))
                            ready = (s_ac, cE)
                            if j >= 8 * qc:
                                cMk = p.tt(prT[pb][:, :], prT[pb][:, :],
                                           maskM[j - 8 * qc][:, :], ALU.mult,
                                           waits=[ready], inc=v_ev)
                                ready = (v_ev, cMk)
                            if j + 2 < nj:
                                mm2(j + 2)
                            if j == 1:
                                flush_factor()
                            p.wait("tensor", *ready)
                            p.mm(pst[db][0:1, 0:512], ones128[:, :],
                                 prT[pb][:, :], j == 0, j == nj - 1)
                            last_av = p.mm(pst[ab][:, :],
                                           Vh[ph][:, 128 * j:128 * (j + 1)],
                                           prT[pb][:, :], j == 0, j == nj - 1,
                                           inc=t_mm)
                            prT_free[pb] = (t_mm, last_av)
                        pending_factor[0] = (h, qc, db, ab, last_av)
                    head_free[ph] = (t_mm, t_mm.n)
                    if h + 2 < NH:
                        nh_loaded.append(load_head(h + 2))
                    phaseT_step(7)
                flush_factor()
                phaseT_step(len(tspec))
                n_phT = [(c_ph[0], c_ph[0].n), (c_ph[1], c_ph[1].n)]
                attn_t = t_mm.n
                attn_v = v_ev.n
                attn_s = s_ac.n
                wo_ph = [ph_done_evt[k] for k in range(HC)]
                p.waits("sync", wo_ph)
                p.dma("sync",
                      woTcat[:, :].rearrange("p (hc o) -> p hc o", o=H),
                      woT_d.ap().rearrange("p (hc o) -> p hc o", o=H),
                      inc=y_woT)
                n_woT = y_woT.n

            p.scope = "p3_wo_ln1"
            with ExitStack() as st3:
                e3 = st3.enter_context
                xow = [e3(nc.sbuf_tensor(f"xow{i}", [128, H], F32)) for i in range(2)]
                rbuf = [e3(nc.sbuf_tensor(f"rbuf{i}", [128, H], F32)) for i in range(2)]
                jnk = [e3(nc.sbuf_tensor(f"jnk{i}", [128, H], BF16)) for i in range(2)]
                xlnb = [e3(nc.sbuf_tensor(f"xlnb{i}", [128, H], BF16)) for i in range(2)]
                xlnf = [e3(nc.sbuf_tensor(f"xlnf{i}", [128, H], F32)) for i in range(2)]
                p.wait("sync", t_mm, attn_t)
                p.wait("tensor", v_ev, attn_v)

                xow_free = [None, None]
                rbuf_free = [[], []]
                stg3b_free = [None, None]
                stg3f_free = [None, None]
                for stt in range(LT):
                    sp = stt % 2
                    if xow_free[sp] is not None:
                        p.wait("sync", *xow_free[sp])
                        xow_free[sp] = None
                    cx = p.dma("sync", xow[sp][:, :],
                               xo[128 * stt:128 * (stt + 1), :], inc=y_xw[sp])
                    last_ev = None
                    for oc in range(H // 512):
                        b = banks.acquire(p)
                        p.wait("tensor", y_woT, n_woT)
                        last_mm = None
                        for dh in range(HC):
                            last_mm = p.mm(
                                pst[b][:, :],
                                attnT[dh][:, 128 * stt:128 * (stt + 1)],
                                woTcat[:, dh * H + 512 * oc:
                                       dh * H + 512 * (oc + 1)],
                                dh == 0, dh == HC - 1,
                                inc=t_mm if dh == HC - 1 else None)
                        ws = [(t_mm, last_mm), (y_xw[sp], cx)] + rbuf_free[sp]
                        last_ev = p.tt(rbuf[sp][:, 512 * oc:512 * (oc + 1)],
                                       pst[b][:, :],
                                       xow[sp][:, 512 * oc:512 * (oc + 1)],
                                       ALU.add, waits=ws, inc=v_ev)
                        banks.release(b, (v_ev, last_ev))
                    rbuf_free[sp] = []
                    ws_b = [stg3b_free[sp]] if stg3b_free[sp] else []
                    ws_f = [stg3f_free[sp]] if stg3f_free[sp] else []
                    (cb1, cb2), jfree = ln_block(
                        sp, rbuf[sp][:, :], jnk[sp][:, :],
                        [(v_ev, last_ev)],
                        [(xlnb[sp][:, :], ws_b), (xlnf[sp][:, :], ws_f)])
                    rbuf_free[sp] = [(v_ev, cb2), jfree]
                    xow_free[sp] = (v_ev, last_ev)
                    cs1 = p.dma("scalar",
                                xln_bf.ap()[:, 128 * stt:128 * (stt + 1), :]
                                .rearrange("hc p d -> p hc d"),
                                xlnb[sp][:, :].rearrange(
                                    "p (hc d) -> p hc d", d=128),
                                waits=[(v_ev, cb1)], inc=c_x[sp])
                    cs2 = p.dma("scalar",
                                xlnsp.ap()[128 * stt:128 * (stt + 1), :],
                                xlnf[sp][:, :], waits=[(v_ev, cb2)],
                                inc=c_x[sp])
                    stg3b_free[sp] = (c_x[sp], cs2)
                    stg3f_free[sp] = (c_x[sp], cs2)
                n_xln = [(c_x[0], c_x[0].n), (c_x[1], c_x[1].n)]

        p.scope = "p4_ffn"
        p4_barrier = ([(t_mm, t_mm.n), (v_ev, v_ev.n), (s_ac, s_ac.n)]
                      + n_xln + n_phT)
        ITB = IB // 128
        with ExitStack() as st4:
            e4 = st4.enter_context
            acc = [e4(nc.sbuf_tensor(f"acc{i}", [128, H], F32))
                   for i in range(TCK // 128)]
            hsb = [[e4(nc.sbuf_tensor(f"h{pp}_{it}", [128, TCK], BF16))
                    for it in range(ITB)] for pp in range(2)]
            guT = [e4(nc.sbuf_tensor(f"guT{pp}", [128, HC * 2 * IB], BF16))
                   for pp in range(2)]
            wdT = e4(nc.sbuf_tensor("wdT", [128, ITB * H], BF16))
            gact = [e4(nc.sbuf_tensor(f"gact{i}", [128, 512], F32))
                    for i in range(2)]
            ostg = e4(nc.sbuf_tensor("ostg", [128, H], F32))
            xlnT = [e4(nc.sbuf_tensor(f"xlnT{hc}", [128, TCK], BF16))
                    for hc in range(HC)]

            guT_free = [None, None]
            wdT_free = None
            hsb_free = [None, None]
            gact_free = [None, None]
            acc_free = [None] * (TCK // 128)
            xlnT_free = None
            ostg_free = None
            gu_hready = {}

            wguT_r = wguT_d.ap().rearrange("p (hc r o) -> p hc r o",
                                           r=2, o=INTER)
            guT_r = [guT[pp].ap()[:, :].rearrange(
                "p (hc r o) -> p hc r o", r=2, o=IB) for pp in range(2)]

            def load_gu(gb):
                par = gb % 2
                ws = list(n_phT)
                if guT_free[par] is not None:
                    ws.append(guT_free[par])
                return p.dma("sync", guT_r[par],
                             wguT_r[:, :, :, IB * (gb % NB):IB * (gb % NB + 1)],
                             waits=ws, inc=y_w[par])

            def load_wd(gb, wait_evt):
                ib = gb % NB
                ws = list(n_phT)
                if wait_evt is not None:
                    ws.append(wait_evt)
                return p.dma("sync", wdT[:, :],
                             wdT_d.ap()[:, ITB * ib * H:ITB * (ib + 1) * H],
                             waits=ws, inc=y_d)

            def gate_up_block(gb, n_xlnT):
                par = gb % 2
                p.wait("tensor", y_w[par], ngu[gb])
                p.wait("tensor", y_xl, n_xlnT)
                for pr in range(ITB):
                    gb_list = []
                    ub_list = []
                    for half in range(2):
                        for sc2 in range(TCK // 512):
                            b = banks.acquire(p)
                            last_mm = None
                            for hc in range(HC):
                                col0 = hc * 2 * IB + half * IB + 128 * pr
                                last_mm = p.mm(
                                    pst[b][:, :],
                                    guT[par][:, col0:col0 + 128],
                                    xlnT[hc][:, 512 * sc2:512 * (sc2 + 1)],
                                    hc == 0, hc == HC - 1,
                                    inc=t_mm if hc == HC - 1 else None)
                            (gb_list if half == 0 else ub_list).append(
                                (b, last_mm))
                    for sc2 in range(TCK // 512):
                        gbk, cg = gb_list[sc2]
                        ubk, cu = ub_list[sc2]
                        gp = sc2 % 2
                        ws = [(t_mm, cg)]
                        if gact_free[gp] is not None:
                            ws.append(gact_free[gp])
                        ca = p.act(gact[gp][:, :], pst[gbk][:, :],
                                   AF.Silu, waits=ws, inc=s_ac)
                        banks.release(gbk, (s_ac, ca))
                        ws2 = [(s_ac, ca), (t_mm, cu)]
                        if hsb_free[par] is not None:
                            ws2.append(hsb_free[par])
                        chh = p.tt(hsb[par][pr][:, 512 * sc2:512 * (sc2 + 1)],
                                   gact[gp][:, :], pst[ubk][:, :], ALU.mult,
                                   waits=ws2, inc=v_ev)
                        banks.release(ubk, (v_ev, chh))
                        gact_free[gp] = (v_ev, chh)
                gu_hready[gb] = v_ev.n

            def down_block(gb, nwd, n_act):
                par = gb % 2
                p.wait("tensor", y_d, nwd)
                p.wait("tensor", v_ev, gu_hready[gb])
                for st2 in range(TCK // 128):
                    for oc in range(H // 512):
                        b = banks.acquire(p)
                        last_mm = None
                        for it in range(ITB):
                            last_mm = p.mm(
                                pst[b][:, :],
                                hsb[par][it][:, 128 * st2:128 * (st2 + 1)],
                                wdT[:, it * H + 512 * oc:
                                    it * H + 512 * (oc + 1)],
                                it == 0, it == ITB - 1,
                                inc=t_mm if it == ITB - 1 else None)
                        ws = [(t_mm, last_mm), (y_ac, n_act)]
                        if acc_free[st2] is not None:
                            ws.append(acc_free[st2])
                        cv = p.tt(acc[st2][:, 512 * oc:512 * (oc + 1)],
                                  pst[b][:, :],
                                  acc[st2][:, 512 * oc:512 * (oc + 1)],
                                  ALU.add, waits=ws, inc=v_ev)
                        banks.release(b, (v_ev, cv))
                        acc_free[st2] = (v_ev, cv)
                hsb_free[par] = (t_mm, t_mm.n)

            ngu = []
            for chk in range(NCH):
                r0 = chk * TCK
                gb0 = chk * NB
                if chk == 0:
                    p.waits("sync", p4_barrier)
                    p.waits("scalar", p4_barrier)
                else:
                    p.wait("sync", t_mm, xlnT_free[1])
                    p.wait("scalar", t_mm, xlnT_free[1])
                ngu.append(load_gu(gb0))
                for hc in range(HC):
                    p.dma("sync", xlnT[hc][:, :],
                          xln_bf.ap()[hc, r0:r0 + TCK, :],
                          inc=y_xl, transpose=True)
                n_xlnT = y_xl.n
                for i in range(TCK // 128):
                    ws = [acc_free[i]] if acc_free[i] else []
                    p.dma("sync", acc[i][:, :],
                          xlnsp.ap()[r0 + 128 * i:r0 + 128 * (i + 1), :],
                          waits=ws, inc=y_ac)
                    acc_free[i] = None
                n_act = y_ac.n
                nwd0 = load_wd(gb0, wdT_free)
                nwds = {gb0: nwd0}
                for ib in range(NB):
                    gb = gb0 + ib
                    if ib + 1 < NB:
                        ngu.append(load_gu(gb + 1))
                    gate_up_block(gb, n_xlnT)
                    guT_free[gb % 2] = (t_mm, t_mm.n)
                    if ib >= 1:
                        down_block(gb - 1, nwds[gb - 1], n_act)
                        nwds[gb] = load_wd(gb, (t_mm, t_mm.n))
                xlnT_free = (t_mm, t_mm.n)
                down_block(gb0 + NB - 1, nwds[gb0 + NB - 1], n_act)
                wdT_free = (t_mm, t_mm.n)

                p.scope = "p5_ln2_out"
                for st2 in range(TCK // 128):
                    sp = st2 % 2
                    ws_o = [ostg_free] if ostg_free else []
                    (co,), _jf = ln_block(
                        sp, acc[st2][:, :], ostg[:, :],
                        [acc_free[st2]] + ws_o,
                        [(ostg[:, :], [])])
                    cs = p.dma("scalar",
                               out[r0 + 128 * st2:r0 + 128 * (st2 + 1), :],
                               ostg[:, :], waits=[(v_ev, co)],
                               inc=c_o[st2 % 2])
                    ostg_free = (c_o[st2 % 2], cs)
                    acc_free[st2] = (v_ev, co)
                p.scope = "p4_ffn"

        if dbg_on:
            p.waits("gpsimd", n_xln)
            p.dma("gpsimd", dbg_xln, xlnsp.ap(), inc=g_cast)
            p.dma("gpsimd", dbg_xlb, xln_bf.ap(), inc=g_cast)
            p.wait("gpsimd", g_cast, g_cast.n)
        p.scope = "final"
        p.emit("scalar", lambda e, h=c_o[0].h, v=c_o[0].n: e.wait_ge(h, v))
        p.emit("scalar", lambda e, h=c_o[1].h, v=c_o[1].n: e.wait_ge(h, v))

        with nc.Block() as block:
            def replay(eng_ops):
                def _(eng):
                    i = 0
                    n = len(eng_ops)
                    while i < n:
                        scope = eng_ops[i][0]
                        j = i
                        with nc.named_scope(scope):
                            while j < n and eng_ops[j][0] == scope:
                                eng_ops[j][1](eng)
                                j += 1
                        i = j
                return _
            block.tensor(replay(p.ops["tensor"]))
            block.vector(replay(p.ops["vector"]))
            block.scalar(replay(p.ops["scalar"]))
            block.sync(replay(p.ops["sync"]))
            block.gpsimd(replay(p.ops["gpsimd"]))

    return nc



def make_in_maps(cfg, x, w_qkv, w_o, w_gu, w_d):
    cfg.derived()
    in_maps = []
    for c in range(8):
        b, g = c // 2, c % 2
        tiles = zig(g, cfg.LT)
        rows = np.concatenate([np.arange(128 * t, 128 * (t + 1)) for t in tiles])
        in_maps.append({
            "xa": np.ascontiguousarray(x[b]),
            "xo": np.ascontiguousarray(x[b][rows]),
            "wqkv": w_qkv,
            "wo": w_o,
            "wgu": w_gu,
            "wd": w_d,
            "maskM": build_maskM(g),
            "ones1": np.ones((1, 128), dtype=BF16NP),
            "ones128": np.ones((128, 1), dtype=BF16NP),
            "cvec": np.concatenate(
                [np.full((128, 1), -cfg.CSHIFT, np.float32),
                 np.full((128, 1), cfg.EPS, np.float32)], axis=1),
            "ident": np.eye(128, dtype=np.float32),
        })
    return in_maps


def assemble(cfg, results, dtype=np.float32):
    out = np.empty((cfg.B, cfg.S, cfg.H), dtype=dtype)
    for c in range(8):
        b, g = c // 2, c % 2
        tiles = zig(g, cfg.LT)
        res = np.asarray(results[c]["out"])
        for i, t in enumerate(tiles):
            out[b, 128 * t:128 * (t + 1), :] = res[128 * i:128 * (i + 1), :]
    return out


def kernel(L0_attn_hidden_states, L0_attn_w_qkv, L0_attn_w_o,
           L0_ffn_w_gate_up, L0_ffn_w_down):
    cfg = CFG
    cfg.derived()
    x = np.ascontiguousarray(np.asarray(L0_attn_hidden_states, dtype=np.float32))
    wqkv = np.ascontiguousarray(np.asarray(L0_attn_w_qkv, dtype=np.float32))
    wo = np.ascontiguousarray(np.asarray(L0_attn_w_o, dtype=np.float32))
    wgu = np.ascontiguousarray(np.asarray(L0_ffn_w_gate_up, dtype=np.float32))
    wd = np.ascontiguousarray(np.asarray(L0_ffn_w_down, dtype=np.float32))

    nc = build_program(cfg)
    in_maps = make_in_maps(cfg, x, wqkv, wo, wgu, wd)
    res = run_bass_kernel_spmd(nc, in_maps, core_ids=list(range(8)))
    return assemble(cfg, res.results)


# version 21
# speedup vs baseline: 1.1875x; 1.0644x over previous
import numpy as np
import ml_dtypes
from contextlib import ExitStack

import concourse.bass as bass
import concourse.mybir as mybir
from concourse.bass_utils import run_bass_kernel_spmd

F32 = mybir.dt.float32
BF16 = mybir.dt.bfloat16
AF = mybir.ActivationFunctionType
ALU = mybir.AluOpType
BF16NP = ml_dtypes.bfloat16


class CFG:
    S = 4096
    H = 2048
    INTER = 8192
    B = 4
    IB = 512
    TCK0 = 1024
    CSHIFT = 12.0
    EPS = 1e-5

    @classmethod
    def derived(cls):
        cls.SQ = cls.S // 2
        cls.HC = cls.H // 128
        cls.NH = cls.HC
        cls.KT = cls.S // 128
        cls.LT = cls.SQ // 128
        cls.QC = cls.SQ // 512
        cls.NB = cls.INTER // cls.IB
        cls.TCK = min(cls.TCK0, cls.SQ)
        cls.NCH = cls.SQ // cls.TCK


def zig(g, n_local):
    out = []
    for i in range(n_local):
        if g == 0:
            out.append(2 * i + (0 if i % 2 == 0 else 1))
        else:
            out.append(2 * i + (1 if i % 2 == 0 else 0))
    return out


def _par(i, hh):
    return (i % 2) if hh == 0 else (1 - i % 2)


def build_maskM(g):
    m = np.zeros((8, 128, 512), dtype=np.float32)
    for t in range(4):
        rq = 2 * t + _par(t, g)
        for v in range(8):
            rk = 2 * (v // 2) + _par(v // 2, g if v % 2 == 0 else 1 - g)
            blk = m[v][:, 128 * t:128 * (t + 1)]
            if rk < rq:
                blk[:, :] = 1.0
            elif rk == rq:
                kk = np.arange(128)[:, None]
                qq = np.arange(128)[None, :]
                blk[:, :] = (kk <= qq).astype(np.float32)
    return m.astype(BF16NP)



ENGINES = ("tensor", "vector", "scalar", "sync", "gpsimd")


class Sem:
    def __init__(self, h, name):
        self.h = h
        self.name = name
        self.n = 0


class Prog:
    def __init__(self, nc, stack):
        self.nc = nc
        self.stack = stack
        self.ops = {k: [] for k in ENGINES}
        self.waited = {k: {} for k in ENGINES}
        self.scope = "init"

    def sem(self, name):
        return Sem(self.stack.enter_context(self.nc.semaphore(name)), name)

    def emit(self, eng, fn):
        self.ops[eng].append((self.scope, fn))

    def wait(self, eng, sem, val):
        if sem is None or val is None or val <= 0:
            return
        if self.waited[eng].get(sem.name, 0) >= val:
            return
        self.waited[eng][sem.name] = val
        h = sem.h
        self.emit(eng, lambda e, h=h, val=val: e.wait_ge(h, val))

    def waitcur(self, eng, sem):
        self.wait(eng, sem, sem.n)

    def waits(self, eng, evts):
        for ev in evts:
            if ev is not None:
                self.wait(eng, ev[0], ev[1])

    def _run(self, eng, fn, inc, inck):
        if inc is not None:
            inc.n += inck
            self.emit(eng, lambda e, fn=fn, h=inc.h, k=inck: fn(e).then_inc(h, k))
            return inc.n
        self.emit(eng, lambda e, fn=fn: fn(e))
        return None

    def mm(self, out, lhsT, rhs, start, stop, waits=(), inc=None):
        self.waits("tensor", waits)
        return self._run(
            "tensor",
            lambda e, o=out, l=lhsT, r=rhs, st=start, sp=stop:
                e.matmul(o, l, r, start=st, stop=sp),
            inc, 1)

    def act(self, out, in_, func, bias=None, scale=None,
            accum_out=None, waits=(), inc=None):
        self.waits("scalar", waits)
        kw = {}
        if bias is not None:
            kw["bias"] = bias
        if scale is not None:
            kw["scale"] = scale
        if accum_out is not None:
            kw["accum_out"] = accum_out
        return self._run(
            "scalar",
            lambda e, o=out, i=in_, f=func, kw=kw: e.activation(o, i, f, **kw),
            inc, 1)

    def ts(self, out, in0, s1, s2, op0, op1=None, accum_out=None,
           waits=(), inc=None, eng="vector"):
        self.waits(eng, waits)
        kw = {}
        if accum_out is not None:
            kw["accum_out"] = accum_out
        if op1 is not None:
            return self._run(
                eng,
                lambda e, o=out, i=in0, a=s1, b=s2, p0=op0, p1=op1, kw=kw:
                    e.tensor_scalar(o, i, a, b, p0, p1, **kw),
                inc, 1)
        return self._run(
            eng,
            lambda e, o=out, i=in0, a=s1, p0=op0, kw=kw:
                e.tensor_scalar(o, i, a, None, p0, **kw),
            inc, 1)

    def tt(self, out, in0, in1, op, waits=(), inc=None, eng="vector"):
        self.waits(eng, waits)
        return self._run(
            eng,
            lambda e, o=out, a=in0, b=in1, p=op: e.tensor_tensor(o, a, b, p),
            inc, 1)

    def copy(self, out, in_, waits=(), inc=None, eng="vector"):
        self.waits(eng, waits)
        return self._run(
            eng, lambda e, o=out, i=in_: e.tensor_copy(o, i), inc, 1)

    def recip(self, out, in_, waits=(), inc=None):
        self.waits("vector", waits)
        return self._run(
            "vector", lambda e, o=out, i=in_: e.reciprocal(o, i), inc, 1)

    def dma(self, eng, out, in_, waits=(), inc=None, transpose=False,
            max_last=None):
        self.waits(eng, waits)
        kw = {}
        if transpose:
            kw["transpose"] = True
        if max_last is not None:
            kw["max_dma_last_dim"] = max_last
        return self._run(
            eng,
            lambda e, o=out, i=in_, kw=kw: e.dma_start(out=o, in_=i, **kw),
            inc, 16)


class Banks:

    def __init__(self, tensors):
        self.t = tensors
        self.free = [None] * len(tensors)
        self.order = list(range(len(tensors)))

    def acquire(self, p, eng="tensor", lo=0, hi=None):
        hi = len(self.t) if hi is None else hi
        for i, b in enumerate(self.order):
            if lo <= b < hi:
                self.order.pop(i)
                break
        else:
            raise RuntimeError("no bank available in range")
        if self.free[b] is not None:
            p.wait(eng, *self.free[b])
            self.free[b] = None
        return b

    def acquire_idx(self, p, b, eng="tensor"):
        self.order.remove(b)
        if self.free[b] is not None:
            p.wait(eng, *self.free[b])
            self.free[b] = None
        return b

    def release(self, b, evt):
        self.free[b] = evt
        self.order.append(b)



def build_program(cfg):
    cfg.derived()
    S, H, SQ, HC = cfg.S, cfg.H, cfg.SQ, cfg.HC
    INTER, IB, TCK = cfg.INTER, cfg.IB, cfg.TCK
    LT, QC, NB, NCH = cfg.LT, cfg.QC, cfg.NB, cfg.NCH
    CH = H

    nc = bass.Bass("TRN2", target_bir_lowering=False)

    xa = nc.dram_tensor("xa", [S, H], F32, kind="ExternalInput").ap()
    xo = nc.dram_tensor("xo", [SQ, H], F32, kind="ExternalInput").ap()
    wqkv = nc.dram_tensor("wqkv", [3 * CH, H], F32, kind="ExternalInput").ap()
    wo = nc.dram_tensor("wo", [H, H], F32, kind="ExternalInput").ap()
    wgu = nc.dram_tensor("wgu", [2 * INTER, H], F32, kind="ExternalInput").ap()
    wd = nc.dram_tensor("wd", [H, INTER], F32, kind="ExternalInput").ap()
    maskM_in = nc.dram_tensor("maskM", [8, 128, 512], BF16, kind="ExternalInput").ap()
    ones1_in = nc.dram_tensor("ones1", [1, 128], BF16, kind="ExternalInput").ap()
    ones128_in = nc.dram_tensor("ones128", [128, 1], BF16, kind="ExternalInput").ap()
    cvec_in = nc.dram_tensor("cvec", [128, 2], F32, kind="ExternalInput").ap()
    ident_in = nc.dram_tensor("ident", [128, 128], F32, kind="ExternalInput").ap()
    out = nc.dram_tensor("out", [SQ, H], F32, kind="ExternalOutput").ap()
    dbg_on = bool(getattr(CFG, "DEBUG_XLN", False))
    if dbg_on:
        dbg_xln = nc.dram_tensor("dbg_xln", [SQ, H], F32,
                                 kind="ExternalOutput").ap()
        dbg_xlb = nc.dram_tensor("dbg_xlb", [H // 128, SQ, 128], BF16,
                                 kind="ExternalOutput").ap()

    wqkv_cb = nc.dram_tensor("wqkv_cb", [H // 128, 3 * CH, 128], BF16)
    wo_bf = nc.dram_tensor("wo_bf", [H, H], BF16)
    wgu_bf = nc.dram_tensor("wgu_bf", [2 * INTER, H], BF16)
    wd_bf = nc.dram_tensor("wd_bf", [H, INTER], BF16)
    Kd = nc.dram_tensor("Kd", [CH, S], BF16)
    Vs = nc.dram_tensor("Vs", [S, CH], BF16)
    Qd = nc.dram_tensor("Qd", [CH, SQ], BF16)
    xln_bf = nc.dram_tensor("xln_bf", [H // 128, SQ, 128], BF16)
    wguT_d = nc.dram_tensor("wguT_d", [128, (H // 128) * 2 * INTER], BF16)
    wdT_d = nc.dram_tensor("wdT_d", [128, (INTER // 128) * H], BF16)
    woT_d = nc.dram_tensor("woT_d", [128, (H // 128) * H], BF16)
    xlnsp = nc.dram_tensor("xlnsp", [SQ, H], F32)

    with ExitStack() as stack:
        p = Prog(nc, stack)
        e = stack.enter_context

        t_mm = p.sem("t_mm")
        v_ev = p.sem("v_ev")
        s_ac = p.sem("s_ac")
        g_cast = p.sem("g_cast")
        c_k = [p.sem("c_k0"), p.sem("c_k1")]
        c_v = [p.sem("c_v0"), p.sem("c_v1")]
        c_x = [p.sem("c_x0"), p.sem("c_x1")]
        c_o = [p.sem("c_o0"), p.sem("c_o1")]
        y_sm = p.sem("y_sm")
        y_x = p.sem("y_x")
        y_woT = p.sem("y_woT")
        y_xw = [p.sem("y_xw0"), p.sem("y_xw1")]
        y_w = [p.sem("y_w0"), p.sem("y_w1"), p.sem("y_w2")]
        y_h = [p.sem("y_h0"), p.sem("y_h1")]
        y_d = p.sem("y_d")
        y_ph = [p.sem("y_ph0"), p.sem("y_ph1")]
        y_xl = p.sem("y_xl")
        y_ac = p.sem("y_ac")
        c_ph = [p.sem("c_ph0"), p.sem("c_ph1")]

        pst = [e(nc.psum_tensor(f"ps{i}", [128, 512], F32)) for i in range(4)]
        psW = [e(nc.psum_tensor(f"psW{i}", [128, 1024], F32)) for i in range(2)]
        pst = pst + [psW[0][:, 0:512], psW[0][:, 512:1024],
                     psW[1][:, 0:512], psW[1][:, 512:1024]]
        banks = Banks(pst)

        ones1 = e(nc.sbuf_tensor("ones1s", [1, 128], BF16))
        ones128 = e(nc.sbuf_tensor("ones128s", [128, 1], BF16))
        idb = e(nc.sbuf_tensor("idb", [128, 128], F32))
        cvec = e(nc.sbuf_tensor("cvecs", [128, 2], F32))
        negC = cvec[:, 0:1]
        epsv = cvec[:, 1:2]
        lns = {nm: [e(nc.sbuf_tensor(f"{nm}{i}", [128, 1], F32)) for i in range(2)]
               for nm in ("s1v", "s2v", "meanv", "m2v", "varv", "stdv", "rstdv")}
        ln_vfree = [None, None]
        ln_jfree = [None, None]

        def ln_block(sp, src, jnkbuf, src_ready, out_specs):
            ws0 = list(src_ready)
            if ln_vfree[sp] is not None:
                ws0.append(ln_vfree[sp])
            if ln_jfree[sp] is not None:
                ws0.append(ln_jfree[sp])
            c1 = p.act(jnkbuf, src, AF.Copy,
                       accum_out=lns["s1v"][sp][:, :], waits=ws0, inc=s_ac)
            c2 = p.act(jnkbuf, src, AF.Square,
                       accum_out=lns["s2v"][sp][:, :], waits=[(s_ac, c1)],
                       inc=s_ac)
            ln_jfree[sp] = (s_ac, c2)
            wsv = [(s_ac, c1)]
            if ln_vfree[sp] is not None:
                wsv.append(ln_vfree[sp])
            cm = p.ts(lns["meanv"][sp][:, :], lns["s1v"][sp][:, :],
                      1.0 / H, None, ALU.mult, waits=wsv, inc=v_ev)
            cm2 = p.tt(lns["m2v"][sp][:, :], lns["meanv"][sp][:, :],
                       lns["meanv"][sp][:, :], ALU.mult,
                       waits=[(v_ev, cm)], inc=v_ev)
            cva = p.ts(lns["varv"][sp][:, :], lns["s2v"][sp][:, :],
                       1.0 / H, lns["m2v"][sp][:, :], ALU.mult, ALU.subtract,
                       waits=[(s_ac, c2), (v_ev, cm2)], inc=v_ev)
            csd = p.act(lns["stdv"][sp][:, :], lns["varv"][sp][:, :], AF.Sqrt,
                        bias=epsv, waits=[(v_ev, cva)], inc=s_ac)
            crs = p.recip(lns["rstdv"][sp][:, :], lns["stdv"][sp][:, :],
                          waits=[(s_ac, csd)], inc=v_ev)
            outs = []
            prev = crs
            for buf, extra in out_specs:
                ws = [(v_ev, prev)] + list(extra)
                co = p.ts(buf, src, lns["meanv"][sp][:, :],
                          lns["rstdv"][sp][:, :], ALU.subtract, ALU.mult,
                          waits=ws, inc=v_ev)
                outs.append(co)
                prev = co
            ln_vfree[sp] = (v_ev, outs[-1])
            return outs, (s_ac, c2)

        p.scope = "p0_cast"
        def cast_all(dst, src, rows, maxrows=2048):
            p.wait("gpsimd", g_cast, g_cast.n)
            r0 = 0
            while r0 < rows:
                r1 = min(rows, r0 + maxrows)
                p.dma("gpsimd", dst[r0:r1, :], src[r0:r1, :], inc=g_cast,
                      max_last=8192)
                r0 = r1
            return g_cast.n

        p.dma("sync", ones1[:, :], ones1_in, inc=y_sm)
        p.dma("sync", cvec[:, :], cvec_in, inc=y_sm)
        p.dma("sync", ones128[:, :], ones128_in, inc=y_sm)
        p.dma("sync", idb[:, :], ident_in, inc=y_sm)
        n_small = y_sm.n

        def cast_rows(dst, src, r0, r1, maxrows=2048):
            p.wait("gpsimd", g_cast, g_cast.n)
            rr = r0
            while rr < r1:
                r2 = min(r1, rr + maxrows)
                p.dma("gpsimd", dst[rr:r2, :], src[rr:r2, :], inc=g_cast,
                      max_last=8192)
                rr = r2
            return g_cast.n

        def cast_qkv_cb(r0, r1):
            p.wait("gpsimd", g_cast, g_cast.n)
            p.dma("gpsimd",
                  wqkv_cb.ap()[:, r0:r1, :].rearrange("hc r d -> r hc d"),
                  wqkv[r0:r1, :].rearrange("r (hc d) -> r hc d", d=128),
                  inc=g_cast)
            return g_cast.n

        n_qkv_k = [cast_qkv_cb(CH + 512 * i, CH + 512 * (i + 1))
                   for i in range(CH // 512)]
        n_qkv_v = [cast_qkv_cb(2 * CH + 512 * i, 2 * CH + 512 * (i + 1))
                   for i in range(CH // 512)]
        n_qkv_q = [cast_qkv_cb(512 * i, 512 * (i + 1))
                   for i in range(CH // 512)]
        n_wo = cast_all(wo_bf.ap(), wo, H)
        n_gu = cast_all(wgu_bf.ap(), wgu, 2 * INTER)
        n_wd = cast_all(wd_bf.ap(), wd, H, maxrows=1024)

        p.scope = "p1_qkv"
        WSL = 512
        with ExitStack() as st1:
            e1 = st1.enter_context
            NWSL = 3
            wsl = [[e1(nc.sbuf_tensor(f"wsl{pp}_{hc}", [128, WSL], BF16))
                    for hc in range(HC)] for pp in range(NWSL)]
            stg = [e1(nc.sbuf_tensor(f"stg{pp}", [128, S // 2], BF16)) for pp in range(2)]
            stgv = [e1(nc.sbuf_tensor(f"stgv{pp}", [128, 1024], BF16)) for pp in range(2)]
            wsl_free = [None] * NWSL
            stg_free = [None, None]
            stgv_free = [None, None]

            def load_wslab(par, row0, castn):
                if wsl_free[par] is not None:
                    p.wait("sync", *wsl_free[par])
                    wsl_free[par] = None
                p.wait("sync", g_cast, castn)
                for hc in range(HC):
                    p.dma("sync", wsl[par][hc][:, :],
                          wqkv_cb.ap()[hc, row0:row0 + WSL, :],
                          inc=y_w[par], transpose=True)
                return y_w[par].n

            with ExitStack() as st1a:
                e1a = st1a.enter_context
                xT = [e1a(nc.sbuf_tensor(f"xT{hc}", [128, S], BF16))
                      for hc in range(HC)]
                xan = [e1a(nc.sbuf_tensor(f"xan{i}", [128, H], F32))
                       for i in range(2)]
                g_xa = [p.sem("g_xa0"), p.sem("g_xa1")]
                xan_free = [None, None]

                def transpose_in(src, n_tiles, dst, g_sem):
                    p.wait("tensor", y_sm, n_small)
                    for st in range(n_tiles):
                        par = st % 2
                        ws = [xan_free[par]] if xan_free[par] else []
                        cgx = p.dma("sync", xan[par][:, :],
                                    src[128 * st:128 * (st + 1), :],
                                    waits=ws, inc=g_sem[par])
                        p.wait("tensor", g_sem[par], cgx)
                        last = None
                        for hc in range(HC):
                            b = banks.acquire(p)
                            pslot = pst[b][:, 0:128]
                            ctr = p._run(
                                "tensor",
                                lambda e, o=pslot,
                                i=xan[par][:, 128 * hc:128 * (hc + 1)],
                                idm=idb[:, :]: e.transpose(o, i, idm),
                                t_mm, 1)
                            dstap = dst[hc][:, 128 * st:128 * (st + 1)]
                            if hc % 2 == 0:
                                cev = p.copy(dstap, pslot,
                                             waits=[(t_mm, ctr)], inc=v_ev)
                                banks.release(b, (v_ev, cev))
                            else:
                                cev = p.act(dstap, pslot, AF.Copy,
                                            waits=[(t_mm, ctr)], inc=s_ac)
                                banks.release(b, (s_ac, cev))
                            last = ctr
                        xan_free[par] = (t_mm, last)
                    return (v_ev.n, s_ac.n)

                nxe_v, nxe_s = transpose_in(xa, S // 128, xT, g_xa)
                p.wait("tensor", v_ev, nxe_v)
                p.wait("tensor", s_ac, nxe_s)

                for sl in range(CH // WSL):
                    par = sl % NWSL
                    nw = load_wslab(par, CH + WSL * sl, n_qkv_k[sl])
                    p.wait("tensor", y_w[par], nw)
                    last_mm = None
                    for ct in range(WSL // 128):
                        gct = sl * (WSL // 128) + ct
                        for hf in range(2):
                            sp = hf
                            if stg_free[sp] is not None:
                                p.wait("vector", *stg_free[sp])
                                stg_free[sp] = None
                            last_ev = None
                            for sc in range(S // 1024):
                                b = banks.acquire(p)
                                c0 = hf * (S // 2) + 512 * sc
                                for hc in range(HC):
                                    last_mm = p.mm(
                                        pst[b][:, :],
                                        wsl[par][hc][:, 128 * ct:128 * (ct + 1)],
                                        xT[hc][:, c0:c0 + 512],
                                        hc == 0, hc == HC - 1,
                                        inc=t_mm if hc == HC - 1 else None)
                                last_ev = p.ts(
                                    stg[sp][:, 512 * sc:512 * (sc + 1)],
                                    pst[b][:, :], 1.0, None, ALU.mult,
                                    waits=[(t_mm, last_mm)], inc=v_ev)
                                banks.release(b, (v_ev, last_ev))
                            cd = p.dma("scalar",
                                       Kd.ap()[128 * gct:128 * (gct + 1),
                                               hf * (S // 2):(hf + 1) * (S // 2)],
                                       stg[sp][:, :],
                                       waits=[(v_ev, last_ev)], inc=c_k[sp])
                            stg_free[sp] = (c_k[sp], cd)
                    wsl_free[par] = (t_mm, last_mm)

                for sl in range(CH // WSL):
                    par = (CH // WSL + sl) % NWSL
                    nw = load_wslab(par, 2 * CH + WSL * sl, n_qkv_v[sl])
                    p.wait("tensor", y_w[par], nw)
                    last_mm = None
                    for stt in range(S // 128):
                        grp = stt // 2
                        sp = grp % 2
                        sub = stt % 2
                        if sub == 0 and stgv_free[sp] is not None:
                            p.wait("vector", *stgv_free[sp])
                            stgv_free[sp] = None
                        b = banks.acquire(p)
                        for hc in range(HC):
                            last_mm = p.mm(
                                pst[b][:, :],
                                xT[hc][:, 128 * stt:128 * (stt + 1)],
                                wsl[par][hc][:, :],
                                hc == 0, hc == HC - 1,
                                inc=t_mm if hc == HC - 1 else None)
                        cv = p.ts(stgv[sp][:, 512 * sub:512 * (sub + 1)],
                                  pst[b][:, :], 1.0, None,
                                  ALU.mult, waits=[(t_mm, last_mm)], inc=v_ev)
                        banks.release(b, (v_ev, cv))
                        if sub == 1:
                            cd = p.dma("scalar",
                                       Vs.ap()[128 * (stt - 1):128 * (stt + 1),
                                               WSL * sl:WSL * (sl + 1)]
                                       .rearrange("(t p) v -> p t v", p=128),
                                       stgv[sp][:, :].rearrange(
                                           "p (t v) -> p t v", v=WSL),
                                       waits=[(v_ev, cv)], inc=c_v[sp])
                            stgv_free[sp] = (c_v[sp], cd)
                    wsl_free[par] = (t_mm, last_mm)
                qscale = 1.0 / float(np.sqrt(128.0))
                xTq = [xT[hc][:, :].rearrange("p (a two c) -> p a two c",
                                              two=2, c=128)
                       for hc in range(HC)]
                for sl in range(CH // WSL):
                    par = (2 * (CH // WSL) + sl) % NWSL
                    nw = load_wslab(par, WSL * sl, n_qkv_q[sl])
                    p.wait("tensor", y_w[par], nw)
                    last_mm = None
                    for ct in range(WSL // 128):
                        gct = sl * (WSL // 128) + ct
                        sp = gct % 2
                        if stg_free[sp] is not None:
                            p.wait("vector", *stg_free[sp])
                            stg_free[sp] = None
                        last_ev = None
                        for sc in range(SQ // 512):
                            b = banks.acquire(p)
                            for hc in range(HC):
                                last_mm = p.mm(
                                    pst[b][:, :],
                                    wsl[par][hc][:, 128 * ct:128 * (ct + 1)],
                                    xTq[hc][:, 4 * sc:4 * (sc + 1), 0, :],
                                    hc == 0, hc == HC - 1,
                                    inc=t_mm if hc == HC - 1 else None)
                            last_ev = p.ts(stg[sp][:, 512 * sc:512 * (sc + 1)],
                                           pst[b][:, :], qscale, None, ALU.mult,
                                           waits=[(t_mm, last_mm)], inc=v_ev)
                            banks.release(b, (v_ev, last_ev))
                        cd = p.dma("scalar",
                                   Qd.ap()[128 * gct:128 * (gct + 1), :],
                                   stg[sp][:, :], waits=[(v_ev, last_ev)],
                                   inc=c_k[sp])
                        stg_free[sp] = (c_k[sp], cd)
                    wsl_free[par] = (t_mm, last_mm)
                q_done_mm = t_mm.n
            n_spill = [(c_k[0], c_k[0].n), (c_k[1], c_k[1].n),
                       (c_v[0], c_v[0].n), (c_v[1], c_v[1].n)]

        p.scope = "p2_attn"
        NH = HC
        with ExitStack() as st2a:
            e2a = st2a.enter_context
            attnT = [e2a(nc.sbuf_tensor(f"attnT{dh}", [128, SQ], BF16))
                     for dh in range(NH)]
            woTcat = e2a(nc.sbuf_tensor("woTcat", [128, HC * H], BF16))

            with ExitStack() as st2b:
                e2b = st2b.enter_context
                p1_done_pre = [(t_mm, q_done_mm), (v_ev, v_ev.n),
                               (s_ac, s_ac.n)] + n_spill
                Qh = [e2b(nc.sbuf_tensor(f"Qh{i}", [128, SQ], BF16)) for i in range(2)]
                Kh = [e2b(nc.sbuf_tensor(f"Kh{i}", [128, S], BF16)) for i in range(2)]
                Vh = [e2b(nc.sbuf_tensor(f"Vh{i}", [128, S], BF16)) for i in range(2)]
                NPRT = 4
                prT = [e2b(nc.sbuf_tensor(f"prT{i}", [128, 1024], BF16))
                       for i in range(NPRT)]
                maskM = [e2b(nc.sbuf_tensor(f"maskM{v}", [128, 512], BF16))
                         for v in range(8)]
                frow_f = e2b(nc.sbuf_tensor("frow_f", [1, 512], F32))
                frow_b = e2b(nc.sbuf_tensor("frow_b", [1, 512], BF16))
                fb_sb = e2b(nc.sbuf_tensor("fb_sb", [128, 512], F32))
                p.waits("sync", p1_done_pre)
                for v in range(8):
                    p.dma("sync", maskM[v][:, :], maskM_in[v], inc=y_sm)
                n_small = y_sm.n
                stg_ph = [e2b(nc.sbuf_tensor(f"stg_ph{i}", [128, 4096], BF16))
                          for i in range(2)]
                head_free = [None, None]
                prT_free = [None] * NPRT
                frow_f_free = None
                frow_b_free = None
                fb_sb_free = None

                p1_done = [(t_mm, q_done_mm), (v_ev, v_ev.n)] + n_spill

                tspec = []
                for hc in range(HC):
                    tspec.append((wo_bf.ap()[:, 128 * hc:128 * (hc + 1)],
                                  woT_d.ap()[:, hc * H:(hc + 1) * H],
                                  H, n_wo))
                GUC = min(4096, 2 * INTER)
                for hc in range(HC):
                    for ck in range((2 * INTER) // GUC):
                        tspec.append((wgu_bf.ap()[GUC * ck:GUC * (ck + 1),
                                                  128 * hc:128 * (hc + 1)],
                                      wguT_d.ap()[:, hc * 2 * INTER + GUC * ck:
                                                  hc * 2 * INTER + GUC * (ck + 1)],
                                      GUC, n_gu))
                for itg in range(INTER // 128):
                    tspec.append((wd_bf.ap()[:, 128 * itg:128 * (itg + 1)],
                                  wdT_d.ap()[:, itg * H:(itg + 1) * H],
                                  H, n_wd))
                ph_store_cnt = [0, 0]
                ph_i = [0]
                ph_done_evt = {}

                def phaseT_step(nsteps):
                    for _ in range(nsteps):
                        k = ph_i[0]
                        if k >= len(tspec):
                            return
                        ph_i[0] = k + 1
                        src, dst, rows, castn = tspec[k]
                        par = k % 2
                        ws = [(g_cast, castn)]
                        if ph_store_cnt[par]:
                            ws.append((c_ph[par], ph_store_cnt[par]))
                        ct = p.dma("sync", stg_ph[par][:, 0:rows], src,
                                   transpose=True, waits=ws, inc=y_ph[par])
                        cs = p.dma("gpsimd", dst, stg_ph[par][:, 0:rows],
                                   waits=[(y_ph[par], ct)], inc=c_ph[par])
                        ph_store_cnt[par] = cs
                        ph_done_evt[k] = (c_ph[par], cs)

                def load_head(h):
                    ph = h % 2
                    if head_free[ph] is not None:
                        p.wait("sync", *head_free[ph])
                    p.waits("sync", p1_done)
                    p.dma("sync", Qh[ph][:, :],
                          Qd.ap()[128 * h:128 * (h + 1), :], inc=y_h[ph])
                    p.dma("sync", Kh[ph][:, :],
                          Kd.ap()[128 * h:128 * (h + 1), :], inc=y_h[ph])
                    p.dma("sync", Vh[ph][:, :].rearrange("p (j d) -> p j d", d=128),
                          Vs.ap()[:, 128 * h:128 * (h + 1)]
                          .rearrange("(j p) d -> p j d", p=128), inc=y_h[ph])
                    return y_h[ph].n

                nh_loaded = [load_head(0), load_head(1)]
                p.wait("vector", y_sm, n_small)
                p.wait("tensor", y_sm, n_small)
                pending_factor = [None]

                def flush_factor():
                    if pending_factor[0] is None:
                        return
                    nonlocal frow_f_free, frow_b_free, fb_sb_free
                    hh, qq, db, ab, last_av = pending_factor[0]
                    pending_factor[0] = None
                    wsr = [(t_mm, last_av)]
                    if frow_f_free is not None:
                        wsr.append(frow_f_free)
                    cR = p.recip(frow_f[0:1, :], pst[db][0:1, 0:512],
                                 waits=wsr, inc=v_ev)
                    banks.release(db, (v_ev, cR))
                    wsc = [(v_ev, cR)]
                    if frow_b_free is not None:
                        wsc.append(frow_b_free)
                    cC = p.copy(frow_b[0:1, :], frow_f[0:1, :],
                                waits=wsc, inc=v_ev)
                    frow_f_free = (v_ev, cC)
                    fbk = banks.acquire(p, hi=4)
                    cF = p.mm(pst[fbk][:, :], ones1[0:1, :], frow_b[0:1, :],
                              True, True, waits=[(v_ev, cC)], inc=t_mm)
                    frow_b_free = (t_mm, cF)
                    wsb = [(t_mm, cF)]
                    if fb_sb_free is not None:
                        wsb.append(fb_sb_free)
                    cB = p.copy(fb_sb[:, :], pst[fbk][:, :],
                                waits=wsb, inc=v_ev)
                    banks.release(fbk, (v_ev, cB))
                    cT = p.tt(attnT[hh][:, 512 * qq:512 * (qq + 1)],
                              pst[ab][:, :], fb_sb[:, :], ALU.mult,
                              waits=[(v_ev, cB)], inc=v_ev)
                    banks.release(ab, (v_ev, cT))
                    fb_sb_free = (v_ev, cT)

                for h in range(NH):
                    ph = h % 2
                    p.wait("tensor", y_h[ph], nh_loaded[h])
                    for qc in range(QC):
                        db = banks.acquire(p, hi=4)
                        ab = banks.acquire(p, hi=4)
                        nj = 8 * qc + 8
                        npair = nj // 2
                        sbanks = {}

                        def mm2pair(k):
                            for cand in (4, 6):
                                if cand in banks.order and cand + 1 in banks.order:
                                    w = cand
                                    break
                            else:
                                raise RuntimeError("no psW pair free")
                            banks.acquire_idx(p, w)
                            banks.acquire_idx(p, w + 1)
                            banks_take[0] += 1
                            c0 = p.mm(pst[w][:, :],
                                      Kh[ph][:, 256 * k:256 * k + 128],
                                      Qh[ph][:, 512 * qc:512 * (qc + 1)],
                                      True, True)
                            c1 = p.mm(pst[w + 1][:, :],
                                      Kh[ph][:, 256 * k + 128:256 * (k + 1)],
                                      Qh[ph][:, 512 * qc:512 * (qc + 1)],
                                      True, True, inc=t_mm)
                            sbanks[k] = (w, c1)

                        banks_take = [0]
                        mm2pair(0)
                        last_av = None
                        for k in range(npair):
                            w, cS = sbanks.pop(k)
                            pb = k % NPRT
                            ws = [(t_mm, cS)]
                            if prT_free[pb] is not None:
                                ws.append(prT_free[pb])
                            cE = p.act(prT[pb][:, :],
                                       psW[(w - 4) // 2][:, :],
                                       AF.Exp, bias=negC,
                                       waits=ws, inc=s_ac)
                            banks.release(w, (s_ac, cE))
                            banks.release(w + 1, (s_ac, cE))
                            if k + 1 < npair:
                                mm2pair(k + 1)
                            if k == 0:
                                flush_factor()
                            for dj in range(2):
                                j = 2 * k + dj
                                half = prT[pb][:, 512 * dj:512 * (dj + 1)]
                                ready = (s_ac, cE)
                                if j >= 8 * qc:
                                    cMk = p.tt(half, half,
                                               maskM[j - 8 * qc][:, :],
                                               ALU.mult, waits=[ready],
                                               inc=v_ev)
                                    ready = (v_ev, cMk)
                                p.wait("tensor", *ready)
                                p.mm(pst[db][0:1, 0:512], ones128[:, :],
                                     half, j == 0, j == nj - 1)
                                last_av = p.mm(pst[ab][:, :],
                                               Vh[ph][:, 128 * j:128 * (j + 1)],
                                               half, j == 0, j == nj - 1,
                                               inc=t_mm)
                            prT_free[pb] = (t_mm, last_av)
                        pending_factor[0] = (h, qc, db, ab, last_av)
                    head_free[ph] = (t_mm, t_mm.n)
                    if h + 2 < NH:
                        nh_loaded.append(load_head(h + 2))
                    phaseT_step(7)
                flush_factor()
                phaseT_step(len(tspec))
                n_phT = [(c_ph[0], c_ph[0].n), (c_ph[1], c_ph[1].n)]
                attn_t = t_mm.n
                attn_v = v_ev.n
                attn_s = s_ac.n
                wo_ph = [ph_done_evt[k] for k in range(HC)]
                p.waits("sync", wo_ph)
                p.dma("sync",
                      woTcat[:, :].rearrange("p (hc o) -> p hc o", o=H),
                      woT_d.ap().rearrange("p (hc o) -> p hc o", o=H),
                      inc=y_woT)
                n_woT = y_woT.n

            p.scope = "p3_wo_ln1"
            with ExitStack() as st3:
                e3 = st3.enter_context
                xow = [e3(nc.sbuf_tensor(f"xow{i}", [128, H], F32)) for i in range(2)]
                rbuf = [e3(nc.sbuf_tensor(f"rbuf{i}", [128, H], F32)) for i in range(2)]
                jnk = [e3(nc.sbuf_tensor(f"jnk{i}", [128, H], BF16)) for i in range(2)]
                xlnb = [e3(nc.sbuf_tensor(f"xlnb{i}", [128, H], BF16)) for i in range(2)]
                xlnf = [e3(nc.sbuf_tensor(f"xlnf{i}", [128, H], F32)) for i in range(2)]
                p.wait("sync", t_mm, attn_t)
                p.wait("tensor", v_ev, attn_v)

                xow_free = [None, None]
                rbuf_free = [[], []]
                stg3b_free = [None, None]
                stg3f_free = [None, None]
                for stt in range(LT):
                    sp = stt % 2
                    if xow_free[sp] is not None:
                        p.wait("sync", *xow_free[sp])
                        xow_free[sp] = None
                    cx = p.dma("sync", xow[sp][:, :],
                               xo[128 * stt:128 * (stt + 1), :], inc=y_xw[sp])
                    last_ev = None
                    for oc in range(H // 512):
                        b = banks.acquire(p)
                        p.wait("tensor", y_woT, n_woT)
                        last_mm = None
                        for dh in range(HC):
                            last_mm = p.mm(
                                pst[b][:, :],
                                attnT[dh][:, 128 * stt:128 * (stt + 1)],
                                woTcat[:, dh * H + 512 * oc:
                                       dh * H + 512 * (oc + 1)],
                                dh == 0, dh == HC - 1,
                                inc=t_mm if dh == HC - 1 else None)
                        ws = [(t_mm, last_mm), (y_xw[sp], cx)] + rbuf_free[sp]
                        last_ev = p.tt(rbuf[sp][:, 512 * oc:512 * (oc + 1)],
                                       pst[b][:, :],
                                       xow[sp][:, 512 * oc:512 * (oc + 1)],
                                       ALU.add, waits=ws, inc=v_ev)
                        banks.release(b, (v_ev, last_ev))
                    rbuf_free[sp] = []
                    ws_b = [stg3b_free[sp]] if stg3b_free[sp] else []
                    ws_f = [stg3f_free[sp]] if stg3f_free[sp] else []
                    (cb1, cb2), jfree = ln_block(
                        sp, rbuf[sp][:, :], jnk[sp][:, :],
                        [(v_ev, last_ev)],
                        [(xlnb[sp][:, :], ws_b), (xlnf[sp][:, :], ws_f)])
                    rbuf_free[sp] = [(v_ev, cb2), jfree]
                    xow_free[sp] = (v_ev, last_ev)
                    cs1 = p.dma("scalar",
                                xln_bf.ap()[:, 128 * stt:128 * (stt + 1), :]
                                .rearrange("hc p d -> p hc d"),
                                xlnb[sp][:, :].rearrange(
                                    "p (hc d) -> p hc d", d=128),
                                waits=[(v_ev, cb1)], inc=c_x[sp])
                    cs2 = p.dma("scalar",
                                xlnsp.ap()[128 * stt:128 * (stt + 1), :],
                                xlnf[sp][:, :], waits=[(v_ev, cb2)],
                                inc=c_x[sp])
                    stg3b_free[sp] = (c_x[sp], cs2)
                    stg3f_free[sp] = (c_x[sp], cs2)
                n_xln = [(c_x[0], c_x[0].n), (c_x[1], c_x[1].n)]

        p.scope = "p4_ffn"
        p4_barrier = ([(t_mm, t_mm.n), (v_ev, v_ev.n), (s_ac, s_ac.n)]
                      + n_xln + n_phT)
        ITB = IB // 128
        with ExitStack() as st4:
            e4 = st4.enter_context
            acc = [e4(nc.sbuf_tensor(f"acc{i}", [128, H], F32))
                   for i in range(TCK // 128)]
            hsb = [[e4(nc.sbuf_tensor(f"h{pp}_{it}", [128, TCK], BF16))
                    for it in range(ITB)] for pp in range(2)]
            guT = [e4(nc.sbuf_tensor(f"guT{pp}", [128, HC * 2 * IB], BF16))
                   for pp in range(2)]
            wdT = e4(nc.sbuf_tensor("wdT", [128, ITB * H], BF16))
            gact = [e4(nc.sbuf_tensor(f"gact{i}", [128, 512], F32))
                    for i in range(2)]
            ostg = e4(nc.sbuf_tensor("ostg", [128, H], F32))
            xlnT = [e4(nc.sbuf_tensor(f"xlnT{hc}", [128, TCK], BF16))
                    for hc in range(HC)]

            guT_free = [None, None]
            wdT_free = None
            hsb_free = [None, None]
            gact_free = [None, None]
            acc_free = [None] * (TCK // 128)
            xlnT_free = None
            ostg_free = None
            gu_hready = {}

            wguT_r = wguT_d.ap().rearrange("p (hc r o) -> p hc r o",
                                           r=2, o=INTER)
            guT_r = [guT[pp].ap()[:, :].rearrange(
                "p (hc r o) -> p hc r o", r=2, o=IB) for pp in range(2)]

            def load_gu(gb):
                par = gb % 2
                ws = list(n_phT)
                if guT_free[par] is not None:
                    ws.append(guT_free[par])
                return p.dma("sync", guT_r[par],
                             wguT_r[:, :, :, IB * (gb % NB):IB * (gb % NB + 1)],
                             waits=ws, inc=y_w[par])

            def load_wd(gb, wait_evt):
                ib = gb % NB
                ws = list(n_phT)
                if wait_evt is not None:
                    ws.append(wait_evt)
                return p.dma("sync", wdT[:, :],
                             wdT_d.ap()[:, ITB * ib * H:ITB * (ib + 1) * H],
                             waits=ws, inc=y_d)

            def gate_up_block(gb, n_xlnT):
                par = gb % 2
                p.wait("tensor", y_w[par], ngu[gb])
                p.wait("tensor", y_xl, n_xlnT)
                for pr in range(ITB):
                    gb_list = []
                    ub_list = []
                    for half in range(2):
                        for sc2 in range(TCK // 512):
                            b = banks.acquire(p)
                            last_mm = None
                            for hc in range(HC):
                                col0 = hc * 2 * IB + half * IB + 128 * pr
                                last_mm = p.mm(
                                    pst[b][:, :],
                                    guT[par][:, col0:col0 + 128],
                                    xlnT[hc][:, 512 * sc2:512 * (sc2 + 1)],
                                    hc == 0, hc == HC - 1,
                                    inc=t_mm if hc == HC - 1 else None)
                            (gb_list if half == 0 else ub_list).append(
                                (b, last_mm))
                    for sc2 in range(TCK // 512):
                        gbk, cg = gb_list[sc2]
                        ubk, cu = ub_list[sc2]
                        gp = sc2 % 2
                        ws = [(t_mm, cg)]
                        if gact_free[gp] is not None:
                            ws.append(gact_free[gp])
                        ca = p.act(gact[gp][:, :], pst[gbk][:, :],
                                   AF.Silu, waits=ws, inc=s_ac)
                        banks.release(gbk, (s_ac, ca))
                        ws2 = [(s_ac, ca), (t_mm, cu)]
                        if hsb_free[par] is not None:
                            ws2.append(hsb_free[par])
                        chh = p.tt(hsb[par][pr][:, 512 * sc2:512 * (sc2 + 1)],
                                   gact[gp][:, :], pst[ubk][:, :], ALU.mult,
                                   waits=ws2, inc=v_ev)
                        banks.release(ubk, (v_ev, chh))
                        gact_free[gp] = (v_ev, chh)
                gu_hready[gb] = v_ev.n

            def down_block(gb, nwd, n_act):
                par = gb % 2
                p.wait("tensor", y_d, nwd)
                p.wait("tensor", v_ev, gu_hready[gb])
                for st2 in range(TCK // 128):
                    for oc in range(H // 512):
                        b = banks.acquire(p)
                        last_mm = None
                        for it in range(ITB):
                            last_mm = p.mm(
                                pst[b][:, :],
                                hsb[par][it][:, 128 * st2:128 * (st2 + 1)],
                                wdT[:, it * H + 512 * oc:
                                    it * H + 512 * (oc + 1)],
                                it == 0, it == ITB - 1,
                                inc=t_mm if it == ITB - 1 else None)
                        ws = [(t_mm, last_mm), (y_ac, n_act)]
                        if acc_free[st2] is not None:
                            ws.append(acc_free[st2])
                        cv = p.tt(acc[st2][:, 512 * oc:512 * (oc + 1)],
                                  pst[b][:, :],
                                  acc[st2][:, 512 * oc:512 * (oc + 1)],
                                  ALU.add, waits=ws, inc=v_ev)
                        banks.release(b, (v_ev, cv))
                        acc_free[st2] = (v_ev, cv)
                hsb_free[par] = (t_mm, t_mm.n)

            ngu = []
            for chk in range(NCH):
                r0 = chk * TCK
                gb0 = chk * NB
                if chk == 0:
                    p.waits("sync", p4_barrier)
                    p.waits("scalar", p4_barrier)
                else:
                    p.wait("sync", t_mm, xlnT_free[1])
                    p.wait("scalar", t_mm, xlnT_free[1])
                ngu.append(load_gu(gb0))
                for hc in range(HC):
                    p.dma("sync", xlnT[hc][:, :],
                          xln_bf.ap()[hc, r0:r0 + TCK, :],
                          inc=y_xl, transpose=True)
                n_xlnT = y_xl.n
                for i in range(TCK // 128):
                    ws = [acc_free[i]] if acc_free[i] else []
                    p.dma("sync", acc[i][:, :],
                          xlnsp.ap()[r0 + 128 * i:r0 + 128 * (i + 1), :],
                          waits=ws, inc=y_ac)
                    acc_free[i] = None
                n_act = y_ac.n
                nwd0 = load_wd(gb0, wdT_free)
                nwds = {gb0: nwd0}
                for ib in range(NB):
                    gb = gb0 + ib
                    if ib + 1 < NB:
                        ngu.append(load_gu(gb + 1))
                    gate_up_block(gb, n_xlnT)
                    guT_free[gb % 2] = (t_mm, t_mm.n)
                    if ib >= 1:
                        down_block(gb - 1, nwds[gb - 1], n_act)
                        nwds[gb] = load_wd(gb, (t_mm, t_mm.n))
                xlnT_free = (t_mm, t_mm.n)
                down_block(gb0 + NB - 1, nwds[gb0 + NB - 1], n_act)
                wdT_free = (t_mm, t_mm.n)

                p.scope = "p5_ln2_out"
                for st2 in range(TCK // 128):
                    sp = st2 % 2
                    ws_o = [ostg_free] if ostg_free else []
                    (co,), _jf = ln_block(
                        sp, acc[st2][:, :], ostg[:, :],
                        [acc_free[st2]] + ws_o,
                        [(ostg[:, :], [])])
                    cs = p.dma("scalar",
                               out[r0 + 128 * st2:r0 + 128 * (st2 + 1), :],
                               ostg[:, :], waits=[(v_ev, co)],
                               inc=c_o[st2 % 2])
                    ostg_free = (c_o[st2 % 2], cs)
                    acc_free[st2] = (v_ev, co)
                p.scope = "p4_ffn"

        if dbg_on:
            p.waits("gpsimd", n_xln)
            p.dma("gpsimd", dbg_xln, xlnsp.ap(), inc=g_cast)
            p.dma("gpsimd", dbg_xlb, xln_bf.ap(), inc=g_cast)
            p.wait("gpsimd", g_cast, g_cast.n)
        p.scope = "final"
        p.emit("scalar", lambda e, h=c_o[0].h, v=c_o[0].n: e.wait_ge(h, v))
        p.emit("scalar", lambda e, h=c_o[1].h, v=c_o[1].n: e.wait_ge(h, v))

        with nc.Block() as block:
            def replay(eng_ops):
                def _(eng):
                    i = 0
                    n = len(eng_ops)
                    while i < n:
                        scope = eng_ops[i][0]
                        j = i
                        with nc.named_scope(scope):
                            while j < n and eng_ops[j][0] == scope:
                                eng_ops[j][1](eng)
                                j += 1
                        i = j
                return _
            block.tensor(replay(p.ops["tensor"]))
            block.vector(replay(p.ops["vector"]))
            block.scalar(replay(p.ops["scalar"]))
            block.sync(replay(p.ops["sync"]))
            block.gpsimd(replay(p.ops["gpsimd"]))

    return nc



def make_in_maps(cfg, x, w_qkv, w_o, w_gu, w_d):
    cfg.derived()
    in_maps = []
    for c in range(8):
        b, g = c // 2, c % 2
        tiles = zig(g, cfg.LT)
        rows = np.concatenate([np.arange(128 * t, 128 * (t + 1)) for t in tiles])
        ptiles = []
        other = zig(1 - g, cfg.LT)
        for i in range(cfg.LT):
            ptiles += [tiles[i], other[i]]
        prows = np.concatenate([np.arange(128 * t, 128 * (t + 1))
                                for t in ptiles])
        in_maps.append({
            "xa": np.ascontiguousarray(x[b][prows]),
            "xo": np.ascontiguousarray(x[b][rows]),
            "wqkv": w_qkv,
            "wo": w_o,
            "wgu": w_gu,
            "wd": w_d,
            "maskM": build_maskM(g),
            "ones1": np.ones((1, 128), dtype=BF16NP),
            "ones128": np.ones((128, 1), dtype=BF16NP),
            "cvec": np.concatenate(
                [np.full((128, 1), -cfg.CSHIFT, np.float32),
                 np.full((128, 1), cfg.EPS, np.float32)], axis=1),
            "ident": np.eye(128, dtype=np.float32),
        })
    return in_maps


def assemble(cfg, results, dtype=np.float32):
    out = np.empty((cfg.B, cfg.S, cfg.H), dtype=dtype)
    for c in range(8):
        b, g = c // 2, c % 2
        tiles = zig(g, cfg.LT)
        res = np.asarray(results[c]["out"])
        for i, t in enumerate(tiles):
            out[b, 128 * t:128 * (t + 1), :] = res[128 * i:128 * (i + 1), :]
    return out


def kernel(L0_attn_hidden_states, L0_attn_w_qkv, L0_attn_w_o,
           L0_ffn_w_gate_up, L0_ffn_w_down):
    cfg = CFG
    cfg.derived()
    x = np.ascontiguousarray(np.asarray(L0_attn_hidden_states, dtype=np.float32))
    wqkv = np.ascontiguousarray(np.asarray(L0_attn_w_qkv, dtype=np.float32))
    wo = np.ascontiguousarray(np.asarray(L0_attn_w_o, dtype=np.float32))
    wgu = np.ascontiguousarray(np.asarray(L0_ffn_w_gate_up, dtype=np.float32))
    wd = np.ascontiguousarray(np.asarray(L0_ffn_w_down, dtype=np.float32))

    nc = build_program(cfg)
    in_maps = make_in_maps(cfg, x, wqkv, wo, wgu, wd)
    res = run_bass_kernel_spmd(nc, in_maps, core_ids=list(range(8)))
    return assemble(cfg, res.results)
